# version 01
# baseline (speedup 1.0000x reference)
import numpy as np
from contextlib import ExitStack

import concourse.bass as bass
import concourse.mybir as mybir
from concourse.bass_utils import run_bass_kernel_spmd

"""Host-side input prep shared by dev scripts and kernel.py."""

_DENSE_LOCAL_W = 64
_DYADIC_ALL = [96, 128, 192, 256, 384, 512, 768, 1024, 1536]
OFFSETS = sorted(set(range(0, _DENSE_LOCAL_W + 1)) | set(_DYADIC_ALL))
WIN_OFF = set(range(0, 65)) | {96, 128, 192}
DYADIC = [256, 384, 512, 768, 1024, 1536]
H = 16
NEG = -1e30


def make_bwin(pos_bias):
    pb = {o: pos_bias[m] for m, o in enumerate(OFFSETS)}
    out = np.full((H, 3, 128, 128), NEG, dtype=np.float32)
    for c in range(3):
        base = (2 - c) * 128
        for j in range(128):
            for i in range(128):
                off = base + i - j
                if off in WIN_OFF:
                    out[:, c, j, i] = pb[off]
    return out


def make_bdyn(pos_bias):
    idx = {o: m for m, o in enumerate(OFFSETS)}
    v = np.empty((len(DYADIC), H), dtype=np.float32)
    for mi, o in enumerate(DYADIC):
        v[mi, :] = pos_bias[idx[o]]
    row = v.reshape(-1)
    return np.broadcast_to(row, (128, H * len(DYADIC))).copy()


def core_inputs(ins, n_cores=8):
    x = np.ascontiguousarray(ins["L0_attn_x"], dtype=np.float32)
    bwin = np.ascontiguousarray(make_bwin(np.asarray(ins["L0_attn_pos_bias"])))
    bdyn = np.ascontiguousarray(make_bdyn(np.asarray(ins["L0_attn_pos_bias"])))
    com = {
        "wg": np.ascontiguousarray(ins["L0_ffn_w_gate"], dtype=np.float32),
        "wu": np.ascontiguousarray(ins["L0_ffn_w_up"], dtype=np.float32),
        "wd": np.ascontiguousarray(ins["L0_ffn_w_down"], dtype=np.float32),
        "wqkv": np.ascontiguousarray(ins["L0_attn_Wqkv"], dtype=np.float32),
        "bqkvb": np.ascontiguousarray(
            np.broadcast_to(ins["L0_attn_bqkv"], (128, 3072))).astype(np.float32),
        "wgate": np.ascontiguousarray(ins["L0_attn_Wgate"], dtype=np.float32),
        "bgateb": np.ascontiguousarray(
            np.broadcast_to(ins["L0_attn_bgate"], (128, 1024))).astype(np.float32),
        "wout": np.ascontiguousarray(ins["L0_attn_Wout"], dtype=np.float32),
        "boutb": np.ascontiguousarray(
            np.broadcast_to(ins["L0_attn_bout"], (128, 1024))).astype(np.float32),
        "bwin": bwin,
        "bdyn": bdyn,
    }
    per = x.shape[0] // n_cores
    maps = []
    for c in range(n_cores):
        m = dict(com)
        m["x"] = np.ascontiguousarray(x[c * per:(c + 1) * per])
        maps.append(m)
    return maps


"""Kernel-builder helpers: per-engine semaphore bookkeeping + dependency tracking.

Compute instructions increment their engine's progress semaphore (+1 each).
DMA queues use R rotating semaphores with issue-gating: a DMA on rotation
slot j may only issue once the previous DMA on slot j completed, so each
slot's semaphore is serialized and every 16*m boundary is a deterministic
wait value (concurrent DMAs on ONE semaphore post their 16 increments
interleaved, making intermediate waits racy).
"""

NDMASEM = 6


class Buf:
    __slots__ = ("name", "writer", "readers")

    def __init__(self, name):
        self.name = name
        self.writer = None
        self.readers = []


class Eng:
    def __init__(self, kb, name, eng, sem, dma_sems):
        self.kb = kb
        self.name = name
        self.e = eng
        self.sem = sem
        self.dma_sems = dma_sems
        self.dma_counts = [0] * len(dma_sems)
        self.dma_idx = 0
        self.n = 0
        self.known = {}

    def _wait(self, token):
        sem, val = token
        if self.name == "pe" and sem is self.sem:
            return
        if self.known.get(id(sem), -1) >= val:
            return
        self.e.wait_ge(sem, val)
        self.known[id(sem)] = val


class KB:
    def __init__(self, nc, es):
        self.nc = nc
        mk = lambda nm: es.enter_context(nc.semaphore(nm))
        self.pe = Eng(self, "pe", nc.tensor, mk("sem_pe"), [])
        self.dve = Eng(self, "dve", nc.vector, mk("sem_dve"), [])
        self.act = Eng(self, "act", nc.scalar, mk("sem_act"),
                       [mk(f"dsc{i}") for i in range(NDMASEM)])
        self.pool = Eng(self, "pool", nc.gpsimd, mk("sem_pool"),
                        [mk(f"dg{i}") for i in range(NDMASEM)])
        self.sy = Eng(self, "sy", nc.sync, mk("sem_sy"),
                      [mk(f"dsy{i}") for i in range(NDMASEM)])
        self.engines = [self.pe, self.dve, self.act, self.pool, self.sy]

    def op(self, eng, method, *args, reads=(), writes=(), **kwargs):
        for b in reads:
            if b.writer is not None:
                eng._wait(b.writer)
        for b in writes:
            if b.writer is not None:
                eng._wait(b.writer)
            for t in b.readers:
                eng._wait(t)
        inst = getattr(eng.e, method)(*args, **kwargs)
        eng.n += 1
        inst.then_inc(eng.sem, 1)
        tok = (eng.sem, eng.n)
        for b in reads:
            b.readers.append(tok)
        for b in writes:
            b.writer = tok
            b.readers = []
        return tok

    def dma(self, eng, dst, src, reads=(), writes=(), transpose=False, **kw):
        for b in reads:
            if b.writer is not None:
                eng._wait(b.writer)
        for b in writes:
            if b.writer is not None:
                eng._wait(b.writer)
            for t in b.readers:
                eng._wait(t)
        j = eng.dma_idx % len(eng.dma_sems)
        eng.dma_idx += 1
        sem = eng.dma_sems[j]
        if eng.dma_counts[j] > 0:
            eng._wait((sem, 16 * eng.dma_counts[j]))
        if transpose:
            inst = eng.e.dma_start_transpose(dst, src)
        else:
            inst = eng.e.dma_start(dst, src, **kw)
        eng.dma_counts[j] += 1
        inst.then_inc(sem, 16)
        tok = (sem, 16 * eng.dma_counts[j])
        for b in reads:
            b.readers.append(tok)
        for b in writes:
            b.writer = tok
            b.readers = []
        return tok

    def barrier_token(self, eng):
        return (eng.sem, eng.n)

    def finalize(self):
        for e in self.engines:
            for j, sem in enumerate(e.dma_sems):
                if e.dma_counts[j] > 0:
                    e._wait((sem, 16 * e.dma_counts[j]))


def full_barrier(kb):
    toks = []
    for e in kb.engines:
        if e.n > 0:
            toks.append((e.sem, e.n))
        for j, sem in enumerate(e.dma_sems):
            if e.dma_counts[j] > 0:
                toks.append((sem, 16 * e.dma_counts[j]))
    for e in kb.engines:
        for t in toks:
            e._wait(t)


"""Megakernel builder for mk_prb_d1: out = x + attn(LN(x)) + ffn(LN(x)).

Per core: n_seq sequences of 2048 tokens. Two weight-phased passes inside ONE
program: pass1 = FFN (fp4 SwiGLU) writing partial = x + ffn + bout to DRAM
scratch (plus xn bf16 cache); pass2 = sparse attention reading partial/xn and
writing out.
"""

F32 = mybir.dt.float32
BF16 = mybir.dt.bfloat16
F16 = mybir.dt.float16
ALU = mybir.AluOpType
AX = mybir.AxisListType
AF = mybir.ActivationFunctionType

D = 1024
N = 2048
I = 2816
ICHUNKS = [(0, 512), (512, 512), (1024, 512), (1536, 512), (2048, 512), (2560, 256)]
H = 16
HD = 64
WCH = 3
DYADIC = [256, 384, 512, 768, 1024, 1536]
ND = len(DYADIC)
EPS = 1e-5
QW = 1408


def emit_ln(kb, t, b, x_ap, xnb_ap, bx, bxnb):
    dve, act = kb.dve, kb.act
    kb.op(dve, "reduce_sum", t["s1"][:], x_ap, AX.X, reads=[bx], writes=[b["s1"]])
    kb.op(dve, "tensor_mul", t["xsq"][:], x_ap, x_ap, reads=[bx], writes=[b["xsq"]])
    kb.op(dve, "reduce_sum", t["s2"][:], t["xsq"][:], AX.X, reads=[b["xsq"]], writes=[b["s2"]])
    kb.op(dve, "tensor_scalar", t["mean"][:], t["s1"][:], 1.0 / D, None, ALU.mult,
          reads=[b["s1"]], writes=[b["mean"]])
    kb.op(dve, "tensor_mul", t["tmp1"][:], t["mean"][:], t["mean"][:],
          reads=[b["mean"]], writes=[b["tmp1"]])
    kb.op(dve, "scalar_tensor_tensor", t["var"][:], t["s2"][:], 1.0 / D, t["tmp1"][:],
          ALU.mult, ALU.subtract, reads=[b["s2"], b["tmp1"]], writes=[b["var"]])
    kb.op(dve, "tensor_scalar", t["var"][:], t["var"][:], 1.0, EPS, ALU.mult, ALU.add,
          reads=[b["var"]], writes=[b["var"]])
    kb.op(dve, "memset", t["y"][:], 1.0, writes=[b["y"]])
    for _ in range(4):
        kb.op(dve, "tensor_mul", t["tmp1"][:], t["y"][:], t["y"][:],
              reads=[b["y"]], writes=[b["tmp1"]])
        kb.op(dve, "tensor_mul", t["tmp1"][:], t["tmp1"][:], t["var"][:],
              reads=[b["tmp1"], b["var"]], writes=[b["tmp1"]])
        kb.op(dve, "tensor_scalar", t["tmp1"][:], t["tmp1"][:], -0.5, 1.5, ALU.mult, ALU.add,
              reads=[b["tmp1"]], writes=[b["tmp1"]])
        kb.op(dve, "tensor_mul", t["y"][:], t["y"][:], t["tmp1"][:],
              reads=[b["y"], b["tmp1"]], writes=[b["y"]])
    kb.op(dve, "scalar_tensor_tensor", t["nmr"][:], t["mean"][:], -1.0, t["y"][:],
          ALU.mult, ALU.mult, reads=[b["mean"], b["y"]], writes=[b["nmr"]])
    kb.op(act, "activation", xnb_ap, x_ap, AF.Identity, bias=t["nmr"][:], scale=t["y"][:],
          reads=[bx, b["nmr"], b["y"]], writes=[bxnb])


def emit_quant(kb, t, b, xs_ap, xq_ap, bxs, bxq, width):
    dve, act, pool = kb.dve, kb.act, kb.pool
    nb = width // 16
    W = width
    v3 = lambda ap: ap.rearrange("p (a c) -> p a c", c=16)
    xa, qa, qb_, qd, qh = (t["qxa"][:, :W], t["qa"][:, :W], t["qb"][:, :W],
                           t["qd"][:, :W], t["qh"][:, :W])
    m, mc, r6f = t["qm"][:, :nb], t["qmc"][:, :nb], t["qr6f"][:, :nb]
    r6h, sbh = t["qr6h"][:, :nb], t["qsbh"][:, :nb]
    B = {k: b["q_" + k] for k in ("xa", "m", "mc", "r6f", "r6h", "sbh",
                                  "a", "b", "d", "h")}
    kb.op(act, "activation", xa, xs_ap, AF.Abs, reads=[bxs], writes=[B["xa"]])
    kb.op(dve, "reduce_max", m, v3(xa), AX.X, reads=[B["xa"]], writes=[B["m"]])
    kb.op(dve, "tensor_scalar", mc, m, 1e-4, None, ALU.max, reads=[B["m"]], writes=[B["mc"]])
    kb.op(dve, "tensor_scalar", sbh, mc, 1.0 / 6.0, None, ALU.mult,
          reads=[B["mc"]], writes=[B["sbh"]])
    kb.op(dve, "reciprocal", r6f, mc, reads=[B["mc"]], writes=[B["r6f"]])
    kb.op(dve, "tensor_scalar", r6h, r6f, 6.0, None, ALU.mult,
          reads=[B["r6f"]], writes=[B["r6h"]])
    bc = r6h.unsqueeze(2).broadcast_to([128, nb, 16])
    kb.op(dve, "tensor_tensor", v3(xa), v3(xa), bc, ALU.mult,
          reads=[B["xa"], B["r6h"]], writes=[B["xa"]])
    u = xa
    kb.op(pool, "tensor_scalar", qa, u, 0.25, None, ALU.is_gt,
          reads=[B["xa"]], writes=[B["a"]])
    kb.op(dve, "scalar_tensor_tensor", qb_, u, 0.75, qa, ALU.is_gt, ALU.add,
          reads=[B["xa"], B["a"]], writes=[B["b"]])
    kb.op(dve, "scalar_tensor_tensor", qa, u, 1.25, qb_, ALU.is_gt, ALU.add,
          reads=[B["xa"], B["b"]], writes=[B["a"]])
    kb.op(dve, "scalar_tensor_tensor", qb_, u, 1.75, qa, ALU.is_gt, ALU.add,
          reads=[B["xa"], B["a"]], writes=[B["b"]])
    kb.op(pool, "tensor_scalar", qd, u, 2.5, None, ALU.is_gt,
          reads=[B["xa"]], writes=[B["d"]])
    kb.op(dve, "scalar_tensor_tensor", qd, u, 3.5, qd, ALU.is_gt, ALU.add,
          reads=[B["xa"], B["d"]], writes=[B["d"]])
    kb.op(pool, "tensor_scalar", qh, u, 5.0, 2.0, ALU.is_gt, ALU.mult,
          reads=[B["xa"]], writes=[B["h"]])
    kb.op(dve, "scalar_tensor_tensor", qb_, qb_, 0.5, qd, ALU.mult, ALU.add,
          reads=[B["b"], B["d"]], writes=[B["b"]])
    kb.op(act, "activation", qd, xs_ap, AF.Sign, reads=[bxs], writes=[B["d"]])
    sb_bc = sbh.unsqueeze(2).broadcast_to([128, nb, 16])
    kb.op(dve, "tensor_tensor", v3(qd), v3(qd), sb_bc, ALU.mult,
          reads=[B["d"], B["sbh"]], writes=[B["d"]])
    kb.op(dve, "tensor_add", qa, qb_, qh, reads=[B["b"], B["h"]], writes=[B["a"]])
    kb.op(dve, "tensor_mul", xq_ap, qa, qd, reads=[B["a"], B["d"]], writes=[bxq])


def build(n_seq=2, nt=16, only_pass1=False, dbg=None, no_dyadic=False, no_c0=False):
    nc = bass.Bass("TRN2", target_bir_lowering=False)
    es = ExitStack()
    kb = KB(nc, es)
    sbt = lambda name, shape, dt=BF16: es.enter_context(nc.sbuf_tensor(name, shape, dt))

    x_d = nc.dram_tensor("x", [n_seq, N, D], F32, kind="ExternalInput").ap()
    out_d = nc.dram_tensor("out", [n_seq, N, D], F32, kind="ExternalOutput").ap()
    part_d = nc.dram_tensor("part", [n_seq, N, D], F32, kind="Internal").ap()
    xnb_d = nc.dram_tensor("xnbc", [n_seq, N, D], BF16, kind="Internal").ap()
    wg_d = nc.dram_tensor("wg", [I, D], F32, kind="ExternalInput").ap()
    wu_d = nc.dram_tensor("wu", [I, D], F32, kind="ExternalInput").ap()
    wd_d = nc.dram_tensor("wd", [D, I], F32, kind="ExternalInput").ap()
    wqkv_d = nc.dram_tensor("wqkv", [3 * D, D], F32, kind="ExternalInput").ap()
    bqkv_d = nc.dram_tensor("bqkvb", [128, 3 * D], F32, kind="ExternalInput").ap()
    wgt_d = nc.dram_tensor("wgate", [D, D], F32, kind="ExternalInput").ap()
    bgt_d = nc.dram_tensor("bgateb", [128, D], F32, kind="ExternalInput").ap()
    wo_d = nc.dram_tensor("wout", [D, D], F32, kind="ExternalInput").ap()
    bo_d = nc.dram_tensor("boutb", [128, D], F32, kind="ExternalInput").ap()
    bwin_d = nc.dram_tensor("bwin", [H, WCH, 128, 128], F32, kind="ExternalInput").ap()
    bdyn_d = nc.dram_tensor("bdyn", [128, H * ND], F32, kind="ExternalInput").ap()

    B = {}
    def buf(nm):
        if nm not in B:
            B[nm] = Buf(nm)
        return B[nm]

    xnb = sbt("xnb", [128, D])
    xnb_fm = sbt("xnb_fm", [128, D])
    small = {nm: sbt("ln_" + nm, [128, 1], F32)
             for nm in ("s1", "s2", "mean", "var", "y", "nmr", "tmp1")}
    part_t = sbt("part_t", [128, D], F32)
    ln_b = {nm: buf(nm) for nm in ("s1", "s2", "mean", "var", "y", "nmr", "tmp1", "xsq")}
    for nm in ("xa", "m", "mc", "r6f", "r6h", "sbh", "a", "b", "d", "h"):
        buf("q_" + nm)
    b_xnb, b_xnbfm, b_part = buf("xnb"), buf("xnb_fm"), buf("part_t")

    pe, dve, act, pool, sy = kb.pe, kb.dve, kb.act, kb.pool, kb.sy

    with ExitStack() as es1:
        s1t = lambda name, shape, dt=BF16: es1.enter_context(nc.sbuf_tensor(name, shape, dt))
        ffnw = s1t("ffnw", [128, 3 * 8 * I])
        WGO, WUO, WDO = 0, 8 * I, 16 * I
        FST = 3 * 8 * I
        wgqT = ffnw[:, WGO:WGO + 8 * I]
        wuqT = ffnw[:, WUO:WUO + 8 * I]
        wdqT = ffnw[:, WDO:WDO + 22 * D]
        b_wgqT, b_wuqT, b_wdqT = buf("wgqT"), buf("wuqT"), buf("wdqT")
        x_t = s1t("x_t", [128, D], F32)
        xnf = s1t("xnf", [128, D], F32)
        xsq = s1t("xsq", [128, D], F32)
        ln_t = dict(small); ln_t["xsq"] = xsq
        b_x = buf("x_t")
        q_t = {}
        for nm in ("qxa", "qa", "qb", "qd", "qh"):
            q_t[nm] = s1t(nm, [128, QW], F16)
        for nm in ("qm", "qmc", "qr6f"):
            q_t[nm] = s1t(nm, [128, QW // 16], F32)
        for nm in ("qr6h", "qsbh"):
            q_t[nm] = s1t(nm, [128, QW // 16], F16)
        xq = s1t("xq", [128, D])
        xq_fm = s1t("xq_fm", [128, D])
        hq = s1t("hq", [128, I])
        hq_fm = s1t("hq_fm", [128, I])
        h_sg = s1t("h_sg", [128, 512], F16)
        h_gt = s1t("h_gt", [128, 512], F16)
        h_up = s1t("h_up", [128, 512], F16)
        h_fl = s1t("h_fl", [128, I], F16)
        wstage = s1t("wstage", [128, D], F32)
        bias_out = s1t("bias_out", [128, D], F32)
        b_xq, b_xqfm = buf("xq"), buf("xq_fm")
        b_hq, b_hqfm, b_hfl = buf("hq"), buf("hq_fm"), buf("h_fl")
        b_hsg, b_hgt, b_hup = buf("h_sg"), buf("h_gt"), buf("h_up")
        b_ws, b_bo = buf("wstage"), buf("bias_out")

        kb.dma(sy, bias_out[:], bo_d, writes=[b_bo])
        for w_d, wTO, bwT, ntile, tw, ost in (
            (wg_d, WGO, b_wgqT, 22, D, I), (wu_d, WUO, b_wuqT, 22, D, I),
            (wd_d, WDO, b_wdqT, 8, I, D),
        ):
            for it in range(ntile):
                for poff in range(0, tw, D):
                    cw = min(D, tw - poff)
                    kb.dma(sy, wstage[:, 0:cw], w_d[it * 128:(it + 1) * 128, poff:poff + cw],
                           writes=[b_ws])
                    emit_quant(kb, q_t, B, wstage[:, 0:cw], hq[:, poff:poff + cw],
                               b_ws, b_hq, cw)
                nblk = tw // 128
                dst = bass.AP(ffnw, wTO + it * 128,
                              [[FST, 128], [ost, nblk], [1, 128]])
                src = hq[:, 0:tw].rearrange("p (c q) -> p c q", c=nblk)
                kb.dma(act, dst, src, transpose=True, reads=[b_hq], writes=[bwT])

        pg = es1.enter_context(nc.psum_tensor("pg", [128, 512], F32))
        pu = es1.enter_context(nc.psum_tensor("pu", [128, 512], F32))
        pd = es1.enter_context(nc.psum_tensor("pd", [128, D], F32))
        b_pg, b_pu, b_pd = buf("pg"), buf("pu"), buf("pd")

        for s in range(n_seq):
            for it in range(nt):
                row = slice(it * 128, (it + 1) * 128)
                kb.dma(sy, x_t[:], x_d[s, row, :], writes=[b_x])
                emit_ln(kb, ln_t, ln_b, x_t[:], xnb[:], b_x, b_xnb)
                kb.dma(sy, xnb_d[s, row, :], xnb[:], reads=[b_xnb], writes=[buf("xnb_d")])
                kb.op(kb.act, "activation", xnf[:], x_t[:], AF.Identity,
                      bias=small["nmr"][:], scale=small["y"][:],
                      reads=[b_x, ln_b["nmr"], ln_b["y"]], writes=[buf("xnf")])
                emit_quant(kb, q_t, B, xnf[:], xq[:], buf("xnf"), b_xq, D)
                kb.dma(act, xq_fm[:].rearrange("p (c q) -> p c q", c=8),
                       xq[:].rearrange("p (c q) -> p c q", c=8),
                       transpose=True, reads=[b_xq], writes=[b_xqfm])
                for (off, cw) in ICHUNKS:
                    for dc in range(8):
                        lhsT = xq_fm[:, dc * 128:(dc + 1) * 128]
                        kb.op(pe, "matmul", pg[:, 0:cw], lhsT,
                              wgqT[:, dc * I + off: dc * I + off + cw],
                              start=(dc == 0), stop=(dc == 7),
                              reads=[b_xqfm, b_wgqT], writes=[b_pg])
                        kb.op(pe, "matmul", pu[:, 0:cw], lhsT,
                              wuqT[:, dc * I + off: dc * I + off + cw],
                              start=(dc == 0), stop=(dc == 7),
                              reads=[b_xqfm, b_wuqT], writes=[b_pu])
                    kb.op(act, "activation", h_sg[:, 0:cw], pg[:, 0:cw], AF.Tanh,
                          scale=0.5, reads=[b_pg], writes=[b_hsg])
                    kb.op(act, "copy", h_gt[:, 0:cw], pg[:, 0:cw],
                          reads=[b_pg], writes=[b_hgt])
                    kb.op(act, "copy", h_up[:, 0:cw], pu[:, 0:cw],
                          reads=[b_pu], writes=[b_hup])
                    kb.op(dve, "tensor_scalar", h_sg[:, 0:cw], h_sg[:, 0:cw], 0.5, 0.5,
                          ALU.mult, ALU.add, reads=[b_hsg], writes=[b_hsg])
                    kb.op(dve, "tensor_mul", h_gt[:, 0:cw], h_gt[:, 0:cw], h_up[:, 0:cw],
                          reads=[b_hgt, b_hup], writes=[b_hgt])
                    kb.op(dve, "tensor_mul", h_fl[:, off:off + cw], h_gt[:, 0:cw],
                          h_sg[:, 0:cw], reads=[b_hgt, b_hsg], writes=[b_hfl])
                emit_quant(kb, q_t, B, h_fl[:, 0:QW], hq[:, 0:QW], b_hfl, b_hq, QW)
                emit_quant(kb, q_t, B, h_fl[:, QW:I], hq[:, QW:I], b_hfl, b_hq, I - QW)
                kb.dma(act, hq_fm[:].rearrange("p (c q) -> p c q", c=22),
                       hq[:].rearrange("p (c q) -> p c q", c=22),
                       transpose=True, reads=[b_hq], writes=[b_hqfm])
                for half in range(2):
                    for ic in range(22):
                        kb.op(pe, "matmul", pd[:, half * 512:(half + 1) * 512],
                              hq_fm[:, ic * 128:(ic + 1) * 128],
                              wdqT[:, ic * D + half * 512: ic * D + (half + 1) * 512],
                              start=(ic == 0), stop=(ic == 21),
                              reads=[b_hqfm, b_wdqT], writes=[b_pd])
                kb.op(dve, "tensor_add", part_t[:], pd[:], x_t[:],
                      reads=[b_pd, b_x], writes=[b_part])
                kb.op(dve, "tensor_add", part_t[:], part_t[:], bias_out[:],
                      reads=[b_part, b_bo], writes=[b_part])
                dst = (out_d if only_pass1 else part_d)[s, row, :]
                kb.dma(sy, dst, part_t[:], reads=[b_part],
                       writes=([] if only_pass1 else [buf("part_d")]))

    if only_pass1:
        kb.finalize()
        es.close()
        return nc

    env = dict(locals()); env['sbt'] = sbt; env['dbg'] = dbg
    env['no_dyadic'] = no_dyadic; env['no_c0'] = no_c0
    build_pass2(nc, es, kb, B, buf, env)
    kb.finalize()
    es.close()
    return nc


def build_pass2(nc, es, kb, B, buf, env):
    sbt = env["sbt"]
    small = env["small"]
    xnb, xnb_fm, part_t = env["xnb"], env["xnb_fm"], env["part_t"]
    x_d, out_d, part_d, xnb_d = env["x_d"], env["out_d"], env["part_d"], env["xnb_d"]
    wqkv_d, bqkv_d = env["wqkv_d"], env["bqkv_d"]
    wgt_d, bgt_d, wo_d = env["wgt_d"], env["bgt_d"], env["wo_d"]
    bwin_d, bdyn_d = env["bwin_d"], env["bdyn_d"]
    n_seq, nt = env["n_seq"], env["nt"]
    dbg = env.get("dbg")
    no_dyadic = env.get("no_dyadic"); no_c0 = env.get("no_c0")
    dbg_d = {}
    if dbg is not None:
        for nm, shape, dt in (("dq", [128, D], BF16), ("dk", [128, D], BF16),
                              ("dv", [128, D], BF16),
                              ("dsd", [128, H * ND], F32), ("ded", [128, H * ND], BF16),
                              ("ddsum", [128, H], F32),
                              ("de4", [128, 4 * WCH * 128], BF16),
                              ("dssum", [128, H], F32), ("dsrec", [128, H], F32),
                              ("dg2", [128, D], BF16), ("dflat", [128, D], BF16),
                              ("dsig", [128, D], BF16), ("dfg", [128, D], BF16)):
            dbg_d[nm] = nc.dram_tensor(nm, shape, dt, kind="ExternalOutput").ap()
    pe, dve, act, pool, sy = kb.pe, kb.dve, kb.act, kb.pool, kb.sy
    b_xnb, b_xnbfm, b_part = B["xnb"], B["xnb_fm"], B["part_t"]
    b_partd, b_xnbd = buf("part_d"), buf("xnb_d")
    NEG = -1e30

    full_barrier(kb)
    with ExitStack() as es2:
        s2t = lambda name, shape, dt=BF16: es2.enter_context(nc.sbuf_tensor(name, shape, dt))
        attnw = s2t("attnw", [128, 8 * 3 * D + 16 * D])
        ATST = 8 * 3 * D + 16 * D
        WQO, WGO2, WOO = 0, 8 * 3 * D, 8 * 3 * D + 8 * D
        wqkvT = attnw[:, WQO:WQO + 24 * D]
        wgT = attnw[:, WGO2:WGO2 + 8 * D]
        woT = attnw[:, WOO:WOO + 8 * D]
        b_wq, b_wg2, b_wo = buf("wqkvT"), buf("wgT"), buf("woT")
        k_tm = s2t("k_tm", [128, 16 * D])
        v_tm = s2t("v_tm", [128, 16 * D])
        kfm = s2t("kfm", [128, 3 * D])
        ebw = s2t("ebw", [128, H * WCH * 128])
        ebd = s2t("ebd", [128, H * ND])
        bias_qkv = s2t("bias_qkv", [128, 3 * D])
        bias_gate = s2t("bias_gate", [128, D])
        ones_col = s2t("ones_col", [128, 1])
        q_tm = s2t("q_tm", [128, D])
        q_fm = s2t("q_fm", [128, D])
        e4 = s2t("e4", [128, 2048])
        sdyn = s2t("sdyn", [128, H * ND], F32)
        edyn = s2t("edyn", [128, H * ND])
        dsum = s2t("dsum", [128, H], F32)
        ssum = s2t("ssum", [128, H], F32)
        srec = s2t("srec", [128, H], F32)
        g1 = s2t("g1", [128, D])
        g2 = s2t("g2", [128, D])
        g3 = s2t("g3", [128, D])
        g4 = s2t("g4", [128, D])
        fl32 = s2t("fl32", [128, D], F32)
        flat_sb = s2t("flat_sb", [128, D])
        fg = s2t("fg", [128, D])
        fg_fm = s2t("fg_fm", [128, D])
        wstage2 = part_t
        b_ws2 = B["part_t"]
        b_ktm = [buf(f"k_tm{j}") for j in range(16)]
        b_vtm = [buf(f"v_tm{j}") for j in range(16)]
        b_kfm = [buf(f"kfm{j}") for j in range(3)]
        b_ebw, b_ebd = buf("ebw"), buf("ebd")
        b_bq, b_bg = buf("bias_qkv"), buf("bias_gate")
        b_ones = buf("ones_col")
        b_qtm, b_qfm = buf("q_tm"), buf("q_fm")
        b_e4, b_sdyn, b_edyn = buf("e4"), buf("sdyn"), buf("edyn")
        b_dsum, b_ssum, b_srec = buf("dsum"), buf("ssum"), buf("srec")
        b_g1, b_g2, b_g3, b_g4 = buf("g1"), buf("g2"), buf("g3"), buf("g4")
        b_fl32, b_flat = buf("fl32"), buf("flat_sb")
        b_fg, b_fgfm = buf("fg"), buf("fg_fm")

        P1 = es2.enter_context(nc.psum_tensor("P1", [128, 2048], F32))
        P2 = es2.enter_context(nc.psum_tensor("P2", [128, D], F32))
        P3 = es2.enter_context(nc.psum_tensor("P3", [128, 512], F32))
        P4 = es2.enter_context(nc.psum_tensor("P4", [128, H], F32))
        b_P1, b_P2, b_P3, b_P4 = buf("P1"), buf("P2"), buf("P3"), buf("P4")

        kb.op(dve, "memset", ones_col[:], 1.0, writes=[b_ones])
        kb.op(dve, "memset", P1[:], 0.0, writes=[b_P1])
        kb.op(dve, "memset", P2[:], 0.0, writes=[b_P2])
        kb.op(dve, "memset", P3[:], 0.0, writes=[b_P3])
        kb.op(dve, "memset", P4[:], 0.0, writes=[b_P4])
        for it in range(24):
            kb.dma(sy, wstage2[:], wqkv_d[it * 128:(it + 1) * 128, :], writes=[b_ws2])
            scale = 0.125 if 8 <= it < 16 else 1.0
            kb.op(act, "activation", xnb[:], wstage2[:], AF.Identity, scale=scale,
                  reads=[b_ws2], writes=[b_xnb])
            dst = bass.AP(attnw, WQO + it * 128, [[ATST, 128], [3 * D, 8], [1, 128]])
            kb.dma(act, dst, xnb[:].rearrange("p (c q) -> p c q", c=8),
                   transpose=True, reads=[b_xnb], writes=[b_wq])
        for w_d, base, bw in ((wgt_d, WGO2, b_wg2), (wo_d, WOO, b_wo)):
            for it in range(8):
                kb.dma(sy, wstage2[:], w_d[it * 128:(it + 1) * 128, :], writes=[b_ws2])
                kb.op(act, "copy", xnb[:], wstage2[:], reads=[b_ws2], writes=[b_xnb])
                dst = bass.AP(attnw, base + it * 128, [[ATST, 128], [D, 8], [1, 128]])
                kb.dma(act, dst, xnb[:].rearrange("p (c q) -> p c q", c=8),
                       transpose=True, reads=[b_xnb], writes=[bw])
        for c in range(3):
            kb.dma(sy, wstage2[:], bqkv_d[:, c * D:(c + 1) * D], writes=[b_ws2])
            kb.op(act, "activation", bias_qkv[:, c * D:(c + 1) * D], wstage2[:],
                  AF.Identity, scale=(0.125 if c == 1 else 1.0),
                  reads=[b_ws2], writes=[b_bq])
        kb.dma(sy, wstage2[:], bgt_d, writes=[b_ws2])
        kb.op(act, "copy", bias_gate[:], wstage2[:], reads=[b_ws2], writes=[b_bg])
        for h in range(H):
            for c in range(WCH):
                kb.dma(sy, wstage2[:, 0:128], bwin_d[h, c], writes=[b_ws2])
                kb.op(act, "activation", ebw[:, (h * WCH + c) * 128:(h * WCH + c + 1) * 128],
                      wstage2[:, 0:128], AF.Exp, reads=[b_ws2], writes=[b_ebw])
        kb.dma(sy, wstage2[:, 0:H * ND], bdyn_d, writes=[b_ws2])
        kb.op(act, "activation", ebd[:], wstage2[:, 0:H * ND], AF.Exp,
              reads=[b_ws2], writes=[b_ebd])

        for s in range(n_seq):
            for it in range(nt):
                row = slice(it * 128, (it + 1) * 128)
                kb.dma(sy, part_t[:], part_d[s, row, :], reads=[b_partd], writes=[b_part])
                kb.dma(sy, xnb[:], xnb_d[s, row, :], reads=[b_xnbd], writes=[b_xnb])
                kb.dma(act, xnb_fm[:].rearrange("p (c q) -> p c q", c=8),
                       xnb[:].rearrange("p (c q) -> p c q", c=8),
                       transpose=True, reads=[b_xnb], writes=[b_xnbfm])
                for g in range(6):
                    for dc in range(8):
                        kb.op(pe, "matmul", P3[:], xnb_fm[:, dc * 128:(dc + 1) * 128],
                              wqkvT[:, dc * 3 * D + g * 512: dc * 3 * D + (g + 1) * 512],
                              start=(dc == 0), stop=(dc == 7),
                              reads=[b_xnbfm, b_wq], writes=[b_P3])
                    if g < 2:
                        dstap, dbuf = q_tm[:, g * 512:(g + 1) * 512], b_qtm
                    elif g < 4:
                        dstap = k_tm[:, it * D + (g - 2) * 512: it * D + (g - 1) * 512]
                        dbuf = b_ktm[it]
                    else:
                        dstap = v_tm[:, it * D + (g - 4) * 512: it * D + (g - 3) * 512]
                        dbuf = b_vtm[it]
                    kb.op(dve, "scalar_tensor_tensor", dstap, P3[:], 1.0,
                          bias_qkv[:, g * 512:(g + 1) * 512], ALU.mult, ALU.add,
                          reads=[b_P3, b_bq], writes=[dbuf])
                if dbg == (s, it):
                    kb.dma(sy, dbg_d["dq"], q_tm[:], reads=[b_qtm])
                    kb.dma(sy, dbg_d["dk"], k_tm[:, it * D:(it + 1) * D], reads=[b_ktm[it]])
                    kb.dma(sy, dbg_d["dv"], v_tm[:, it * D:(it + 1) * D], reads=[b_vtm[it]])
                kb.dma(act, q_fm[:].rearrange("p (c q) -> p c q", c=8),
                       q_tm[:].rearrange("p (c q) -> p c q", c=8),
                       transpose=True, reads=[b_qtm], writes=[b_qfm])
                kb.dma(act, kfm[:, (it % 3) * D:(it % 3 + 1) * D].rearrange("p (c q) -> p c q", c=8),
                       k_tm[:, it * D:(it + 1) * D].rearrange("p (c q) -> p c q", c=8),
                       transpose=True, reads=[b_ktm[it]], writes=[b_kfm[it % 3]])
                kb.op(dve, "memset", sdyn[:], NEG, writes=[b_sdyn])
                for mi, o in enumerate(DYADIC):
                    ot = o // 128
                    if it < ot or no_dyadic:
                        continue
                    kb.op(dve, "tensor_mul", g1[:], q_tm[:],
                          k_tm[:, (it - ot) * D:(it - ot + 1) * D],
                          reads=[b_qtm, b_ktm[it - ot]], writes=[b_g1])
                    kb.op(dve, "reduce_sum", sdyn[:, mi * H:(mi + 1) * H],
                          g1[:].rearrange("p (h d) -> p h d", d=HD), AX.X,
                          reads=[b_g1], writes=[b_sdyn])
                kb.op(act, "activation", edyn[:], sdyn[:], AF.Exp,
                      reads=[b_sdyn], writes=[b_edyn])
                kb.op(dve, "tensor_mul", edyn[:], edyn[:], ebd[:],
                      reads=[b_edyn, b_ebd], writes=[b_edyn])
                ev = bass.AP(edyn, 0, [[edyn[:].ap[0][0], 128], [1, H], [H, ND]])
                kb.op(dve, "reduce_sum", dsum[:], ev, AX.X,
                      reads=[b_edyn], writes=[b_dsum])
                if dbg == (s, it):
                    kb.dma(sy, dbg_d["dsd"], sdyn[:], reads=[b_sdyn])
                    kb.dma(sy, dbg_d["ded"], edyn[:], reads=[b_edyn])
                    kb.dma(sy, dbg_d["ddsum"], dsum[:], reads=[b_dsum])
                for hg in range(4):
                    heads = range(4 * hg, 4 * hg + 4)
                    for hh, h in sorted(enumerate(heads), key=lambda p: p[1] % 2):
                        for c in range(WCH):
                            jt = it - 2 + c
                            reg = P1[:, hh * 512 + c * 128: hh * 512 + (c + 1) * 128]
                            if jt < 0 or (no_c0 and c == 0):
                                kb.op(dve, "memset", reg, NEG, writes=[b_P1])
                                continue
                            pb = (h % 2) * 64
                            lhsT = kfm[pb:pb + 64,
                                       (jt % 3) * D + (h // 2) * 128:(jt % 3) * D + (h // 2) * 128 + 128]
                            rhs = q_fm[pb:pb + 64, (h // 2) * 128:(h // 2) * 128 + 128]
                            kb.op(pe, "matmul", reg, lhsT, rhs, start=True, stop=True,
                                  reads=[b_kfm[jt % 3], b_qfm], writes=[b_P1])
                    kb.op(act, "activation", e4[:], P1[:, 0:2048], AF.Exp,
                          reads=[b_P1], writes=[b_e4])
                    ev4 = e4[:].rearrange("p (a r) -> p a r", r=512)[:, :, 0:WCH * 128]
                    ebv = ebw[:, (4 * hg) * WCH * 128:(4 * hg + 4) * WCH * 128]                        .rearrange("p (a r) -> p a r", r=WCH * 128)
                    kb.op(dve, "tensor_tensor", ev4, ev4, ebv, ALU.mult,
                          reads=[b_e4, b_ebw], writes=[b_e4])
                    if dbg == (s, it) and hg == 0:
                        kb.dma(sy, dbg_d["de4"], e4[:], reads=[b_e4])
                    for hh, h in enumerate(heads):
                        valid = [c for c in range(WCH) if it - 2 + c >= 0 and not (no_c0 and c == 0)]
                        for vi, c in enumerate(valid):
                            jt = it - 2 + c
                            ech = e4[:, hh * 512 + c * 128: hh * 512 + (c + 1) * 128]
                            kb.op(pe, "matmul", P4[:, h:h + 1], ech, ones_col[:],
                                  start=(vi == 0), stop=(vi == len(valid) - 1),
                                  skip_group_check=True,
                                  reads=[b_e4, b_ones], writes=[b_P4])
                            kb.op(pe, "matmul", P2[:, h * HD:(h + 1) * HD], ech,
                                  v_tm[:, jt * D + h * HD: jt * D + (h + 1) * HD],
                                  start=(vi == 0), stop=(vi == len(valid) - 1),
                                  skip_group_check=True,
                                  reads=[b_e4, b_vtm[jt]], writes=[b_P2])
                kb.op(dve, "tensor_add", ssum[:], P4[:], dsum[:],
                      reads=[b_P4, b_dsum], writes=[b_ssum])
                kb.op(dve, "reciprocal", srec[:], ssum[:], reads=[b_ssum], writes=[b_srec])
                if dbg == (s, it):
                    kb.dma(sy, dbg_d["dssum"], ssum[:], reads=[b_ssum])
                    kb.dma(sy, dbg_d["dsrec"], srec[:], reads=[b_srec])
                vd = [] if no_dyadic else [o for o in DYADIC if it >= o // 128]
                if not vd:
                    kb.op(dve, "memset", g2[:], 0.0, writes=[b_g2])
                for vi, o in enumerate(vd):
                    mi = DYADIC.index(o)
                    e_b = edyn[:, mi * H:(mi + 1) * H]
                    e_bc = e_b.unsqueeze(2).broadcast_to([128, H, HD])
                    tgt, tb = (g2, b_g2) if vi == 0 else (g1, b_g1)
                    kb.op(dve, "tensor_tensor",
                          tgt[:].rearrange("p (h d) -> p h d", d=HD),
                          v_tm[:, (it - o // 128) * D:(it - o // 128 + 1) * D]
                          .rearrange("p (h d) -> p h d", d=HD),
                          e_bc, ALU.mult,
                          reads=[b_vtm[it - o // 128], b_edyn], writes=[tb])
                    if vi > 0:
                        kb.op(dve, "tensor_add", g2[:], g2[:], g1[:],
                              reads=[b_g2, b_g1], writes=[b_g2])
                kb.op(dve, "tensor_add", fl32[:], P2[:], g2[:],
                      reads=[b_P2, b_g2], writes=[b_fl32])
                srec_bc = srec[:].unsqueeze(2).broadcast_to([128, H, HD])
                kb.op(dve, "tensor_tensor",
                      flat_sb[:].rearrange("p (h d) -> p h d", d=HD),
                      fl32[:].rearrange("p (h d) -> p h d", d=HD), srec_bc, ALU.mult,
                      reads=[b_fl32, b_srec], writes=[b_flat])
                for g in range(2):
                    for dc in range(8):
                        kb.op(pe, "matmul", P1[:, g * 512:(g + 1) * 512],
                              xnb_fm[:, dc * 128:(dc + 1) * 128],
                              wgT[:, dc * D + g * 512: dc * D + (g + 1) * 512],
                              start=(dc == 0), stop=(dc == 7),
                              reads=[b_xnbfm, b_wg2], writes=[b_P1])
                kb.op(dve, "tensor_add", g3[:], P1[:, 0:D], bias_gate[:],
                      reads=[b_P1, b_bg], writes=[b_g3])
                kb.op(act, "activation", g4[:], g3[:], AF.Tanh, scale=0.5,
                      reads=[b_g3], writes=[b_g4])
                kb.op(dve, "tensor_scalar", g4[:], g4[:], 0.5, 0.5, ALU.mult, ALU.add,
                      reads=[b_g4], writes=[b_g4])
                if dbg == (s, it):
                    kb.dma(sy, dbg_d["dg2"], g2[:], reads=[b_g2])
                    kb.dma(sy, dbg_d["dflat"], flat_sb[:], reads=[b_flat])
                    kb.dma(sy, dbg_d["dsig"], g4[:], reads=[b_g4])
                kb.op(dve, "tensor_mul", fg[:], flat_sb[:], g4[:],
                      reads=[b_flat, b_g4], writes=[b_fg])
                if dbg == (s, it):
                    kb.dma(sy, dbg_d["dfg"], fg[:], reads=[b_fg])
                kb.dma(act, fg_fm[:].rearrange("p (c q) -> p c q", c=8),
                       fg[:].rearrange("p (c q) -> p c q", c=8),
                       transpose=True, reads=[b_fg], writes=[b_fgfm])
                for g in range(2):
                    for dc in range(8):
                        kb.op(pe, "matmul", P2[:, g * 512:(g + 1) * 512],
                              fg_fm[:, dc * 128:(dc + 1) * 128],
                              woT[:, dc * D + g * 512: dc * D + (g + 1) * 512],
                              start=(dc == 0), stop=(dc == 7),
                              reads=[b_fgfm, b_wo], writes=[b_P2])
                kb.op(dve, "tensor_add", part_t[:], P2[:], part_t[:],
                      reads=[b_P2, b_part], writes=[b_part])
                kb.dma(sy, out_d[s, row, :], part_t[:], reads=[b_part])


_NC_CACHE = {}


def kernel(**inputs):
    maps = core_inputs(inputs, n_cores=8)
    if "nc" not in _NC_CACHE:
        _NC_CACHE["nc"] = build(n_seq=2, nt=16, only_pass1=False)
    nc = _NC_CACHE["nc"]
    res = run_bass_kernel_spmd(nc, maps, core_ids=list(range(8)))
    out = np.concatenate([np.asarray(r["out"]) for r in res.results], axis=0)
    return out.astype(np.float32)


# version 02
# speedup vs baseline: 1.0312x; 1.0312x over previous
import numpy as np
from contextlib import ExitStack

import concourse.bass as bass
import concourse.mybir as mybir
from concourse.bass_utils import run_bass_kernel_spmd

"""Host-side input prep shared by dev scripts and kernel.py."""

_DENSE_LOCAL_W = 64
_DYADIC_ALL = [96, 128, 192, 256, 384, 512, 768, 1024, 1536]
OFFSETS = sorted(set(range(0, _DENSE_LOCAL_W + 1)) | set(_DYADIC_ALL))
WIN_OFF = set(range(0, 65)) | {96, 128, 192}
DYADIC = [256, 384, 512, 768, 1024, 1536]
H = 16
NEG = -1e30


def make_bwin(pos_bias):
    pb = {o: pos_bias[m] for m, o in enumerate(OFFSETS)}
    out = np.full((H, 3, 128, 128), NEG, dtype=np.float32)
    for c in range(3):
        base = (2 - c) * 128
        for j in range(128):
            for i in range(128):
                off = base + i - j
                if off in WIN_OFF:
                    out[:, c, j, i] = pb[off]
    return out


def make_bdyn(pos_bias):
    idx = {o: m for m, o in enumerate(OFFSETS)}
    v = np.empty((len(DYADIC), H), dtype=np.float32)
    for mi, o in enumerate(DYADIC):
        v[mi, :] = pos_bias[idx[o]]
    row = v.reshape(-1)
    return np.broadcast_to(row, (128, H * len(DYADIC))).copy()


def core_inputs(ins, n_cores=8):
    x = np.ascontiguousarray(ins["L0_attn_x"], dtype=np.float32)
    bwin = np.ascontiguousarray(make_bwin(np.asarray(ins["L0_attn_pos_bias"])))
    bdyn = np.ascontiguousarray(make_bdyn(np.asarray(ins["L0_attn_pos_bias"])))
    com = {
        "wg": np.ascontiguousarray(ins["L0_ffn_w_gate"], dtype=np.float32),
        "wu": np.ascontiguousarray(ins["L0_ffn_w_up"], dtype=np.float32),
        "wd": np.ascontiguousarray(ins["L0_ffn_w_down"], dtype=np.float32),
        "wqkv": np.ascontiguousarray(ins["L0_attn_Wqkv"], dtype=np.float32),
        "bqkvb": np.ascontiguousarray(
            np.broadcast_to(ins["L0_attn_bqkv"], (128, 3072))).astype(np.float32),
        "wgate": np.ascontiguousarray(ins["L0_attn_Wgate"], dtype=np.float32),
        "bgateb": np.ascontiguousarray(
            np.broadcast_to(ins["L0_attn_bgate"], (128, 1024))).astype(np.float32),
        "wout": np.ascontiguousarray(ins["L0_attn_Wout"], dtype=np.float32),
        "boutb": np.ascontiguousarray(
            np.broadcast_to(ins["L0_attn_bout"], (128, 1024))).astype(np.float32),
        "bwin": bwin,
        "bdyn": bdyn,
    }
    per = x.shape[0] // n_cores
    maps = []
    for c in range(n_cores):
        m = dict(com)
        m["x"] = np.ascontiguousarray(x[c * per:(c + 1) * per])
        maps.append(m)
    return maps


"""Kernel-builder helpers: per-engine semaphore bookkeeping + dependency tracking.

Compute instructions increment their engine's progress semaphore (+1 each).
DMA queues use R rotating semaphores with issue-gating: a DMA on rotation
slot j may only issue once the previous DMA on slot j completed, so each
slot's semaphore is serialized and every 16*m boundary is a deterministic
wait value (concurrent DMAs on ONE semaphore post their 16 increments
interleaved, making intermediate waits racy).
"""

NDMASEM = 6


class Buf:
    __slots__ = ("name", "writer", "readers")

    def __init__(self, name):
        self.name = name
        self.writer = None
        self.readers = []


class Eng:
    def __init__(self, kb, name, eng, sem, dma_sems):
        self.kb = kb
        self.name = name
        self.e = eng
        self.sem = sem
        self.dma_sems = dma_sems
        self.dma_counts = [0] * len(dma_sems)
        self.dma_idx = 0
        self.n = 0
        self.known = {}

    def _wait(self, token):
        sem, val = token
        if self.name == "pe" and sem is self.sem:
            return
        if self.known.get(id(sem), -1) >= val:
            return
        self.e.wait_ge(sem, val)
        self.known[id(sem)] = val


class KB:
    def __init__(self, nc, es):
        self.nc = nc
        mk = lambda nm: es.enter_context(nc.semaphore(nm))
        self.pe = Eng(self, "pe", nc.tensor, mk("sem_pe"), [])
        self.dve = Eng(self, "dve", nc.vector, mk("sem_dve"), [])
        self.act = Eng(self, "act", nc.scalar, mk("sem_act"),
                       [mk(f"dsc{i}") for i in range(NDMASEM)])
        self.pool = Eng(self, "pool", nc.gpsimd, mk("sem_pool"),
                        [mk(f"dg{i}") for i in range(NDMASEM)])
        self.sy = Eng(self, "sy", nc.sync, mk("sem_sy"),
                      [mk(f"dsy{i}") for i in range(NDMASEM)])
        self.engines = [self.pe, self.dve, self.act, self.pool, self.sy]

    def op(self, eng, method, *args, reads=(), writes=(), **kwargs):
        for b in reads:
            if b.writer is not None:
                eng._wait(b.writer)
        for b in writes:
            if b.writer is not None:
                eng._wait(b.writer)
            for t in b.readers:
                eng._wait(t)
        inst = getattr(eng.e, method)(*args, **kwargs)
        eng.n += 1
        inst.then_inc(eng.sem, 1)
        tok = (eng.sem, eng.n)
        for b in reads:
            b.readers.append(tok)
        for b in writes:
            b.writer = tok
            b.readers = []
        return tok

    def dma(self, eng, dst, src, reads=(), writes=(), transpose=False, **kw):
        for b in reads:
            if b.writer is not None:
                eng._wait(b.writer)
        for b in writes:
            if b.writer is not None:
                eng._wait(b.writer)
            for t in b.readers:
                eng._wait(t)
        j = eng.dma_idx % len(eng.dma_sems)
        eng.dma_idx += 1
        sem = eng.dma_sems[j]
        if eng.dma_counts[j] > 0:
            eng._wait((sem, 16 * eng.dma_counts[j]))
        if transpose:
            inst = eng.e.dma_start_transpose(dst, src)
        else:
            inst = eng.e.dma_start(dst, src, **kw)
        eng.dma_counts[j] += 1
        inst.then_inc(sem, 16)
        tok = (sem, 16 * eng.dma_counts[j])
        for b in reads:
            b.readers.append(tok)
        for b in writes:
            b.writer = tok
            b.readers = []
        return tok

    def barrier_token(self, eng):
        return (eng.sem, eng.n)

    def finalize(self):
        for e in self.engines:
            for j, sem in enumerate(e.dma_sems):
                if e.dma_counts[j] > 0:
                    e._wait((sem, 16 * e.dma_counts[j]))


def full_barrier(kb):
    toks = []
    for e in kb.engines:
        if e.n > 0:
            toks.append((e.sem, e.n))
        for j, sem in enumerate(e.dma_sems):
            if e.dma_counts[j] > 0:
                toks.append((sem, 16 * e.dma_counts[j]))
    for e in kb.engines:
        for t in toks:
            e._wait(t)


"""Megakernel builder for mk_prb_d1: out = x + attn(LN(x)) + ffn(LN(x)).

Per core: n_seq sequences of 2048 tokens. Two weight-phased passes inside ONE
program: pass1 = FFN (fp4 SwiGLU) writing partial = x + ffn + bout to DRAM
scratch (plus xn bf16 cache); pass2 = sparse attention reading partial/xn and
writing out.
"""

F32 = mybir.dt.float32
BF16 = mybir.dt.bfloat16
F16 = mybir.dt.float16
ALU = mybir.AluOpType
AX = mybir.AxisListType
AF = mybir.ActivationFunctionType

D = 1024
N = 2048
I = 2816
ICHUNKS = [(0, 512), (512, 512), (1024, 512), (1536, 512), (2048, 512), (2560, 256)]
H = 16
HD = 64
WCH = 3
DYADIC = [256, 384, 512, 768, 1024, 1536]
ND = len(DYADIC)
EPS = 1e-5
QW = 1408


def emit_ln(kb, t, b, x_ap, xnb_ap, bx, bxnb):
    dve, act = kb.dve, kb.act
    kb.op(dve, "reduce_sum", t["s1"][:], x_ap, AX.X, reads=[bx], writes=[b["s1"]])
    kb.op(dve, "tensor_mul", t["xsq"][:], x_ap, x_ap, reads=[bx], writes=[b["xsq"]])
    kb.op(dve, "reduce_sum", t["s2"][:], t["xsq"][:], AX.X, reads=[b["xsq"]], writes=[b["s2"]])
    kb.op(dve, "tensor_scalar", t["mean"][:], t["s1"][:], 1.0 / D, None, ALU.mult,
          reads=[b["s1"]], writes=[b["mean"]])
    kb.op(dve, "tensor_mul", t["tmp1"][:], t["mean"][:], t["mean"][:],
          reads=[b["mean"]], writes=[b["tmp1"]])
    kb.op(dve, "scalar_tensor_tensor", t["var"][:], t["s2"][:], 1.0 / D, t["tmp1"][:],
          ALU.mult, ALU.subtract, reads=[b["s2"], b["tmp1"]], writes=[b["var"]])
    kb.op(dve, "tensor_scalar", t["var"][:], t["var"][:], 1.0, EPS, ALU.mult, ALU.add,
          reads=[b["var"]], writes=[b["var"]])
    kb.op(dve, "memset", t["y"][:], 1.0, writes=[b["y"]])
    for _ in range(4):
        kb.op(dve, "tensor_mul", t["tmp1"][:], t["y"][:], t["y"][:],
              reads=[b["y"]], writes=[b["tmp1"]])
        kb.op(dve, "tensor_mul", t["tmp1"][:], t["tmp1"][:], t["var"][:],
              reads=[b["tmp1"], b["var"]], writes=[b["tmp1"]])
        kb.op(dve, "tensor_scalar", t["tmp1"][:], t["tmp1"][:], -0.5, 1.5, ALU.mult, ALU.add,
              reads=[b["tmp1"]], writes=[b["tmp1"]])
        kb.op(dve, "tensor_mul", t["y"][:], t["y"][:], t["tmp1"][:],
              reads=[b["y"], b["tmp1"]], writes=[b["y"]])
    kb.op(dve, "scalar_tensor_tensor", t["nmr"][:], t["mean"][:], -1.0, t["y"][:],
          ALU.mult, ALU.mult, reads=[b["mean"], b["y"]], writes=[b["nmr"]])
    kb.op(act, "activation", xnb_ap, x_ap, AF.Identity, bias=t["nmr"][:], scale=t["y"][:],
          reads=[bx, b["nmr"], b["y"]], writes=[bxnb])


def emit_quant(kb, t, b, xs_ap, xq_ap, bxs, bxq, width):
    dve, act, pool = kb.dve, kb.act, kb.pool
    nb = width // 16
    W = width
    v3 = lambda ap: ap.rearrange("p (a c) -> p a c", c=16)
    xa, qa, qb_, qd, qh = (t["qxa"][:, :W], t["qa"][:, :W], t["qb"][:, :W],
                           t["qd"][:, :W], t["qh"][:, :W])
    m, mc, r6f = t["qm"][:, :nb], t["qmc"][:, :nb], t["qr6f"][:, :nb]
    r6h, sbh = t["qr6h"][:, :nb], t["qsbh"][:, :nb]
    B = {k: b["q_" + k] for k in ("xa", "m", "mc", "r6f", "r6h", "sbh",
                                  "a", "b", "d", "h")}
    kb.op(act, "activation", xa, xs_ap, AF.Abs, reads=[bxs], writes=[B["xa"]])
    kb.op(dve, "reduce_max", m, v3(xa), AX.X, reads=[B["xa"]], writes=[B["m"]])
    kb.op(dve, "tensor_scalar", mc, m, 1e-4, None, ALU.max, reads=[B["m"]], writes=[B["mc"]])
    kb.op(dve, "tensor_scalar", sbh, mc, 1.0 / 6.0, None, ALU.mult,
          reads=[B["mc"]], writes=[B["sbh"]])
    kb.op(dve, "reciprocal", r6f, mc, reads=[B["mc"]], writes=[B["r6f"]])
    kb.op(dve, "tensor_scalar", r6h, r6f, 6.0, None, ALU.mult,
          reads=[B["r6f"]], writes=[B["r6h"]])
    bc = r6h.unsqueeze(2).broadcast_to([128, nb, 16])
    kb.op(dve, "tensor_tensor", v3(xa), v3(xa), bc, ALU.mult,
          reads=[B["xa"], B["r6h"]], writes=[B["xa"]])
    u = xa
    kb.op(pool, "tensor_scalar", qa, u, 0.25, None, ALU.is_gt,
          reads=[B["xa"]], writes=[B["a"]])
    kb.op(dve, "scalar_tensor_tensor", qb_, u, 0.75, qa, ALU.is_gt, ALU.add,
          reads=[B["xa"], B["a"]], writes=[B["b"]])
    kb.op(dve, "scalar_tensor_tensor", qa, u, 1.25, qb_, ALU.is_gt, ALU.add,
          reads=[B["xa"], B["b"]], writes=[B["a"]])
    kb.op(dve, "scalar_tensor_tensor", qb_, u, 1.75, qa, ALU.is_gt, ALU.add,
          reads=[B["xa"], B["a"]], writes=[B["b"]])
    kb.op(pool, "tensor_scalar", qd, u, 2.5, None, ALU.is_gt,
          reads=[B["xa"]], writes=[B["d"]])
    kb.op(dve, "scalar_tensor_tensor", qd, u, 3.5, qd, ALU.is_gt, ALU.add,
          reads=[B["xa"], B["d"]], writes=[B["d"]])
    kb.op(pool, "tensor_scalar", qh, u, 5.0, 2.0, ALU.is_gt, ALU.mult,
          reads=[B["xa"]], writes=[B["h"]])
    kb.op(dve, "scalar_tensor_tensor", qb_, qb_, 0.5, qd, ALU.mult, ALU.add,
          reads=[B["b"], B["d"]], writes=[B["b"]])
    kb.op(act, "activation", qd, xs_ap, AF.Sign, reads=[bxs], writes=[B["d"]])
    sb_bc = sbh.unsqueeze(2).broadcast_to([128, nb, 16])
    kb.op(dve, "tensor_tensor", v3(qd), v3(qd), sb_bc, ALU.mult,
          reads=[B["d"], B["sbh"]], writes=[B["d"]])
    kb.op(dve, "tensor_add", qa, qb_, qh, reads=[B["b"], B["h"]], writes=[B["a"]])
    kb.op(dve, "tensor_mul", xq_ap, qa, qd, reads=[B["a"], B["d"]], writes=[bxq])


def build(n_seq=2, nt=16, only_pass1=False, dbg=None, no_dyadic=False, no_c0=False):
    nc = bass.Bass("TRN2", target_bir_lowering=False)
    es = ExitStack()
    kb = KB(nc, es)
    sbt = lambda name, shape, dt=BF16: es.enter_context(nc.sbuf_tensor(name, shape, dt))

    x_d = nc.dram_tensor("x", [n_seq, N, D], F32, kind="ExternalInput").ap()
    out_d = nc.dram_tensor("out", [n_seq, N, D], F32, kind="ExternalOutput").ap()
    part_d = nc.dram_tensor("part", [n_seq, N, D], F32, kind="Internal").ap()
    xnb_d = nc.dram_tensor("xnbc", [n_seq, N, D], BF16, kind="Internal").ap()
    wg_d = nc.dram_tensor("wg", [I, D], F32, kind="ExternalInput").ap()
    wu_d = nc.dram_tensor("wu", [I, D], F32, kind="ExternalInput").ap()
    wd_d = nc.dram_tensor("wd", [D, I], F32, kind="ExternalInput").ap()
    wqkv_d = nc.dram_tensor("wqkv", [3 * D, D], F32, kind="ExternalInput").ap()
    bqkv_d = nc.dram_tensor("bqkvb", [128, 3 * D], F32, kind="ExternalInput").ap()
    wgt_d = nc.dram_tensor("wgate", [D, D], F32, kind="ExternalInput").ap()
    bgt_d = nc.dram_tensor("bgateb", [128, D], F32, kind="ExternalInput").ap()
    wo_d = nc.dram_tensor("wout", [D, D], F32, kind="ExternalInput").ap()
    bo_d = nc.dram_tensor("boutb", [128, D], F32, kind="ExternalInput").ap()
    bwin_d = nc.dram_tensor("bwin", [H, WCH, 128, 128], F32, kind="ExternalInput").ap()
    bdyn_d = nc.dram_tensor("bdyn", [128, H * ND], F32, kind="ExternalInput").ap()

    B = {}
    def buf(nm):
        if nm not in B:
            B[nm] = Buf(nm)
        return B[nm]

    xnb = sbt("xnb", [128, D])
    small = {nm: sbt("ln_" + nm, [128, 1], F32)
             for nm in ("s1", "s2", "mean", "var", "y", "nmr", "tmp1")}
    part_t = sbt("part_t", [128, D], F32)
    ln_b = {nm: buf(nm) for nm in ("s1", "s2", "mean", "var", "y", "nmr", "tmp1", "xsq")}
    for nm in ("xa", "m", "mc", "r6f", "r6h", "sbh", "a", "b", "d", "h"):
        buf("q_" + nm)
    b_xnb, b_part = buf("xnb"), buf("part_t")

    pe, dve, act, pool, sy = kb.pe, kb.dve, kb.act, kb.pool, kb.sy

    with ExitStack() as es1:
        s1t = lambda name, shape, dt=BF16: es1.enter_context(nc.sbuf_tensor(name, shape, dt))
        ffnw = s1t("ffnw", [128, 3 * 8 * I])
        WGO, WUO, WDO = 0, 8 * I, 16 * I
        FST = 3 * 8 * I
        wgqT = ffnw[:, WGO:WGO + 8 * I]
        wuqT = ffnw[:, WUO:WUO + 8 * I]
        wdqT = ffnw[:, WDO:WDO + 22 * D]
        b_wgqT, b_wuqT, b_wdqT = buf("wgqT"), buf("wuqT"), buf("wdqT")
        x_t = [s1t(f"x_t{p}", [128, D], F32) for p in range(2)]
        xnf1 = s1t("xnf", [128, D], F32)
        xnf = [xnf1, xnf1]
        xsq = xnf1
        ln_t = dict(small)
        b_x = [buf(f"x_t{p}") for p in range(2)]
        b_xnf1 = buf("xnf")
        b_xnf = [b_xnf1, b_xnf1]
        ln_b["xsq"] = b_xnf1
        q_t = {}
        for nm in ("qxa", "qa", "qb", "qd", "qh"):
            q_t[nm] = s1t(nm, [128, QW], F16)
        for nm in ("qm", "qmc", "qr6f"):
            q_t[nm] = s1t(nm, [128, QW // 16], F32)
        for nm in ("qr6h", "qsbh"):
            q_t[nm] = s1t(nm, [128, QW // 16], F16)
        xq = [s1t(f"xq{p}", [128, D]) for p in range(2)]
        xq_fm = [s1t(f"xq_fm{p}", [128, D]) for p in range(2)]
        hq = [s1t(f"hq{p}", [128, I]) for p in range(2)]
        hq_fm = [s1t(f"hq_fm{p}", [128, I]) for p in range(2)]
        h_sg = s1t("h_sg", [128, 512], F16)
        h_gt = s1t("h_gt", [128, 512], F16)
        h_up = s1t("h_up", [128, 512], F16)
        bias_out = s1t("bias_out", [128, D])
        b_xq = [buf(f"xq{p}") for p in range(2)]
        b_xqfm = [buf(f"xq_fm{p}") for p in range(2)]
        b_hq = [buf(f"hq{p}") for p in range(2)]
        b_hqfm = [buf(f"hq_fm{p}") for p in range(2)]
        b_hsg, b_hgt, b_hup = buf("h_sg"), buf("h_gt"), buf("h_up")
        b_bo = buf("bias_out")
        xnbp = [xnb, xnb]
        b_xnbp = [b_xnb, b_xnb]
        part_p = [part_t, s1t("part2", [128, D], F32)]
        b_partp = [b_part, buf("part2")]
        wstage = x_t[0]
        b_ws = b_x[0]
        kb.dma(sy, xnf1[:], bo_d, writes=[b_xnf1])
        kb.op(act, "copy", bias_out[:], xnf1[:], reads=[b_xnf1], writes=[b_bo])
        for w_d, wTO, bwT, ntile, tw, ost in (
            (wg_d, WGO, b_wgqT, 22, D, I), (wu_d, WUO, b_wuqT, 22, D, I),
            (wd_d, WDO, b_wdqT, 8, I, D),
        ):
            for it in range(ntile):
                wst, bst = x_t[it % 2], b_x[it % 2]
                hqt, bhqt = hq[it % 2], b_hq[it % 2]
                for poff in range(0, tw, D):
                    cw = min(D, tw - poff)
                    kb.dma(sy, wst[:, 0:cw], w_d[it * 128:(it + 1) * 128, poff:poff + cw],
                           writes=[bst])
                    emit_quant(kb, q_t, B, wst[:, 0:cw], hqt[:, poff:poff + cw],
                               bst, bhqt, cw)
                nblk = tw // 128
                dst = bass.AP(ffnw, wTO + it * 128,
                              [[FST, 128], [ost, nblk], [1, 128]])
                src = hqt[:, 0:tw].rearrange("p (c q) -> p c q", c=nblk)
                kb.dma(act, dst, src, transpose=True, reads=[bhqt], writes=[bwT])
        pg = [es1.enter_context(nc.psum_tensor(f"pg{p}", [128, 512], F32)) for p in range(2)]
        pu = [es1.enter_context(nc.psum_tensor(f"pu{p}", [128, 512], F32)) for p in range(2)]
        pd = [es1.enter_context(nc.psum_tensor(f"pd{p}", [128, D], F32)) for p in range(2)]
        b_pg = [buf(f"pg{p}") for p in range(2)]
        b_pu = [buf(f"pu{p}") for p in range(2)]
        b_pd = [buf(f"pd{p}") for p in range(2)]

        for s in range(n_seq):
            for it in range(nt):
                tp = it % 2
                row = slice(it * 128, (it + 1) * 128)
                xt, bx = x_t[tp], b_x[tp]
                xnfp, bxnf = xnf[tp], b_xnf[tp]
                xnbt, bxnbt = xnbp[tp], b_xnbp[tp]
                xqt, bxq = xq[tp], b_xq[tp]
                xqf, bxqf = xq_fm[tp], b_xqfm[tp]
                hqt, bhq = hq[tp], b_hq[tp]
                hqf, bhqf = hq_fm[tp], b_hqfm[tp]
                hfl_ap = hqf[:].bitcast(F16)
                ptt, bpt = part_p[tp], b_partp[tp]
                kb.dma(sy, xt[:], x_d[s, row, :], writes=[bx])
                ln_t["xsq"] = xnfp
                emit_ln(kb, ln_t, ln_b, xt[:], xnbt[:], bx, bxnbt)
                kb.dma(sy, xnb_d[s, row, :], xnbt[:], reads=[bxnbt], writes=[buf("xnb_d")])
                kb.op(kb.act, "activation", xnfp[:], xt[:], AF.Identity,
                      bias=small["nmr"][:], scale=small["y"][:],
                      reads=[bx, ln_b["nmr"], ln_b["y"]], writes=[bxnf])
                emit_quant(kb, q_t, B, xnfp[:], xqt[:], bxnf, bxq, D)
                kb.dma(act, xqf[:].rearrange("p (c q) -> p c q", c=8),
                       xqt[:].rearrange("p (c q) -> p c q", c=8),
                       transpose=True, reads=[bxq], writes=[bxqf])
                for ci, (off, cw) in enumerate(ICHUNKS):
                    cp = ci % 2
                    for dc in range(8):
                        lhsT = xqf[:, dc * 128:(dc + 1) * 128]
                        kb.op(pe, "matmul", pg[cp][:, 0:cw], lhsT,
                              wgqT[:, dc * I + off: dc * I + off + cw],
                              start=(dc == 0), stop=(dc == 7),
                              reads=[bxqf, b_wgqT], writes=[b_pg[cp]])
                        kb.op(pe, "matmul", pu[cp][:, 0:cw], lhsT,
                              wuqT[:, dc * I + off: dc * I + off + cw],
                              start=(dc == 0), stop=(dc == 7),
                              reads=[bxqf, b_wuqT], writes=[b_pu[cp]])
                    kb.op(act, "activation", h_sg[:, 0:cw], pg[cp][:, 0:cw], AF.Tanh,
                          scale=0.5, reads=[b_pg[cp]], writes=[b_hsg])
                    kb.op(act, "copy", h_gt[:, 0:cw], pg[cp][:, 0:cw],
                          reads=[b_pg[cp]], writes=[b_hgt])
                    kb.op(act, "copy", h_up[:, 0:cw], pu[cp][:, 0:cw],
                          reads=[b_pu[cp]], writes=[b_hup])
                    kb.op(dve, "tensor_scalar", h_sg[:, 0:cw], h_sg[:, 0:cw], 0.5, 0.5,
                          ALU.mult, ALU.add, reads=[b_hsg], writes=[b_hsg])
                    kb.op(dve, "tensor_mul", h_gt[:, 0:cw], h_gt[:, 0:cw], h_up[:, 0:cw],
                          reads=[b_hgt, b_hup], writes=[b_hgt])
                    kb.op(dve, "tensor_mul", hfl_ap[:, off:off + cw], h_gt[:, 0:cw],
                          h_sg[:, 0:cw], reads=[b_hgt, b_hsg], writes=[bhqf])
                emit_quant(kb, q_t, B, hfl_ap[:, 0:QW], hqt[:, 0:QW], bhqf, bhq, QW)
                emit_quant(kb, q_t, B, hfl_ap[:, QW:I], hqt[:, QW:I], bhqf, bhq, I - QW)
                kb.dma(act, hqf[:].rearrange("p (c q) -> p c q", c=22),
                       hqt[:].rearrange("p (c q) -> p c q", c=22),
                       transpose=True, reads=[bhq], writes=[bhqf])
                for half in range(2):
                    for ic in range(22):
                        kb.op(pe, "matmul", pd[tp][:, half * 512:(half + 1) * 512],
                              hqf[:, ic * 128:(ic + 1) * 128],
                              wdqT[:, ic * D + half * 512: ic * D + (half + 1) * 512],
                              start=(ic == 0), stop=(ic == 21),
                              reads=[bhqf, b_wdqT], writes=[b_pd[tp]])
                kb.op(dve, "tensor_add", ptt[:], pd[tp][:], xt[:],
                      reads=[b_pd[tp], bx], writes=[bpt])
                kb.op(dve, "tensor_add", ptt[:], ptt[:], bias_out[:],
                      reads=[bpt, b_bo], writes=[bpt])
                dst = (out_d if only_pass1 else part_d)[s, row, :]
                kb.dma(sy, dst, ptt[:], reads=[bpt],
                       writes=([] if only_pass1 else [buf("part_d")]))

    if only_pass1:
        kb.finalize()
        es.close()
        return nc

    env = dict(locals()); env['sbt'] = sbt; env['dbg'] = dbg
    env['no_dyadic'] = no_dyadic; env['no_c0'] = no_c0
    build_pass2(nc, es, kb, B, buf, env)
    kb.finalize()
    es.close()
    return nc


def build_pass2(nc, es, kb, B, buf, env):
    sbt = env["sbt"]
    small = env["small"]
    xnb, part_t = env["xnb"], env["part_t"]
    x_d, out_d, part_d, xnb_d = env["x_d"], env["out_d"], env["part_d"], env["xnb_d"]
    wqkv_d, bqkv_d = env["wqkv_d"], env["bqkv_d"]
    wgt_d, bgt_d, wo_d = env["wgt_d"], env["bgt_d"], env["wo_d"]
    bwin_d, bdyn_d = env["bwin_d"], env["bdyn_d"]
    n_seq, nt = env["n_seq"], env["nt"]
    no_dyadic = env.get("no_dyadic"); no_c0 = env.get("no_c0")
    pe, dve, act, pool, sy = kb.pe, kb.dve, kb.act, kb.pool, kb.sy
    b_partd, b_xnbd = buf("part_d"), buf("xnb_d")
    NEG = -1e30

    with ExitStack() as es2:
        s2t = lambda name, shape, dt=BF16: es2.enter_context(nc.sbuf_tensor(name, shape, dt))
        attnw = s2t("attnw", [128, 8 * 3 * D + 16 * D])
        ATST = 8 * 3 * D + 16 * D
        WQO, WGO2, WOO = 0, 8 * 3 * D, 8 * 3 * D + 8 * D
        wqkvT = attnw[:, WQO:WQO + 24 * D]
        wgT = attnw[:, WGO2:WGO2 + 8 * D]
        woT = attnw[:, WOO:WOO + 8 * D]
        b_wq, b_wg2, b_wo = buf("wqkvT"), buf("wgT"), buf("woT")
        k_tm = s2t("k_tm", [128, 16 * D])
        v_tm = s2t("v_tm", [128, 16 * D])
        kfm = s2t("kfm", [128, 3 * D])
        ebw = s2t("ebw", [128, H * WCH * 128])
        ebd = s2t("ebd", [128, H * ND])
        bias_qkv = s2t("bias_qkv", [128, 3 * D])
        bias_gate = s2t("bias_gate", [128, D])
        ones_col = s2t("ones_col", [128, 1])
        qtm1 = s2t("q_tm", [128, D])
        q_tm = [qtm1, qtm1]
        q_fm = [s2t(f"q_fm{p}", [128, D]) for p in range(2)]
        e4 = [s2t(f"e4{p}", [128, 4 * WCH * 128]) for p in range(2)]
        sdyn = s2t("sdyn", [128, H * ND], F32)
        edyn = s2t("edyn", [128, H * ND])
        dsum = s2t("dsum", [128, H], F32)
        ssum = s2t("ssum", [128, H], F32)
        srec = [s2t(f"srec{p}", [128, H], F32) for p in range(2)]
        g1 = s2t("g1", [128, D])
        g21 = s2t("g2", [128, D])
        g2 = [g21, g21]
        g41 = s2t("g4", [128, D])
        g4 = [g41, g41]
        flat_sb = [s2t(f"flat_sb{p}", [128, D]) for p in range(2)]
        fg1t = s2t("fg", [128, D])
        fg = [fg1t, fg1t]
        fgf1 = s2t("fg_fm", [128, D])
        fg_fm = [fgf1, fgf1]
        xnb2p = [xnb, xnb]
        xnbfm2 = [s2t(f"xnbfm2{p}", [128, D]) for p in range(2)]
        part2p = [part_t, part_t]
        b_part2p = [B["part_t"], B["part_t"]]
        wstage2 = part_t
        b_ws2 = B["part_t"]
        b_ktm = [buf(f"k_tm{j}") for j in range(16)]
        b_vtm = [buf(f"v_tm{j}") for j in range(16)]
        b_kfm = [buf(f"kfm{j}") for j in range(3)]
        b_ebw, b_ebd = buf("ebw"), buf("ebd")
        b_bq, b_bg = buf("bias_qkv"), buf("bias_gate")
        b_ones = buf("ones_col")
        b_qtm1 = buf("q_tm")
        b_qtm = [b_qtm1, b_qtm1]
        b_qfm = [buf(f"q_fm{p}") for p in range(2)]
        b_e4 = [buf(f"e4{p}") for p in range(2)]
        b_sdyn, b_edyn = buf("sdyn"), buf("edyn")
        b_dsum, b_ssum = buf("dsum"), buf("ssum")
        b_srec = [buf(f"srec{p}") for p in range(2)]
        b_g1 = buf("g1")
        b_g21 = buf("g2")
        b_g2 = [b_g21, b_g21]
        b_g41 = buf("g4")
        b_g4 = [b_g41, b_g41]
        b_flat = [buf(f"flat_sb{p}") for p in range(2)]
        b_fg1 = buf("fg")
        b_fg = [b_fg1, b_fg1]
        b_fgf1 = buf("fg_fm")
        b_fgfm = [b_fgf1, b_fgf1]
        b_xnb2p = [B["xnb"], B["xnb"]]
        b_xnbfm2 = [buf(f"xnbfm2{p}") for p in range(2)]

        P1 = es2.enter_context(nc.psum_tensor("P1", [128, 2048], F32))
        P2 = es2.enter_context(nc.psum_tensor("P2", [128, D], F32))
        P3 = es2.enter_context(nc.psum_tensor("P3", [128, 512], F32))
        P4 = es2.enter_context(nc.psum_tensor("P4", [128, 512], F32))
        b_P1b = [buf(f"P1b{b}") for b in range(4)]
        b_P2, b_P3, b_P4 = buf("P2"), buf("P3"), buf("P4")
        qkv_slots = [(P1[:, 0:512], b_P1b[0]), (P1[:, 512:1024], b_P1b[1]),
                     (P4[:, 0:512], b_P4)]

        kb.op(dve, "memset", ones_col[:], 1.0, writes=[b_ones])
        kb.op(dve, "memset", P1[:], 0.0, writes=b_P1b)
        kb.op(dve, "memset", P2[:], 0.0, writes=[b_P2])
        kb.op(dve, "memset", P3[:], 0.0, writes=[b_P3])
        kb.op(dve, "memset", P4[:], 0.0, writes=[b_P4])
        for it in range(24):
            kb.dma(sy, wstage2[:], wqkv_d[it * 128:(it + 1) * 128, :], writes=[b_ws2])
            scale = 0.125 if 8 <= it < 16 else 1.0
            kb.op(act, "activation", xnb[:], wstage2[:], AF.Identity, scale=scale,
                  reads=[b_ws2], writes=[B["xnb"]])
            dst = bass.AP(attnw, WQO + it * 128, [[ATST, 128], [3 * D, 8], [1, 128]])
            kb.dma(act, dst, xnb[:].rearrange("p (c q) -> p c q", c=8),
                   transpose=True, reads=[B["xnb"]], writes=[b_wq])
        for w_d, base, bw in ((wgt_d, WGO2, b_wg2), (wo_d, WOO, b_wo)):
            for it in range(8):
                kb.dma(sy, wstage2[:], w_d[it * 128:(it + 1) * 128, :], writes=[b_ws2])
                kb.op(act, "copy", xnb[:], wstage2[:], reads=[b_ws2], writes=[B["xnb"]])
                dst = bass.AP(attnw, base + it * 128, [[ATST, 128], [D, 8], [1, 128]])
                kb.dma(act, dst, xnb[:].rearrange("p (c q) -> p c q", c=8),
                       transpose=True, reads=[B["xnb"]], writes=[bw])
        for c in range(3):
            kb.dma(sy, wstage2[:, 0:D], bqkv_d[:, c * D:(c + 1) * D], writes=[b_ws2])
            kb.op(act, "activation", bias_qkv[:, c * D:(c + 1) * D], wstage2[:, 0:D],
                  AF.Identity, scale=(0.125 if c == 1 else 1.0),
                  reads=[b_ws2], writes=[b_bq])
        kb.dma(sy, wstage2[:, 0:D], bgt_d, writes=[b_ws2])
        kb.op(act, "copy", bias_gate[:], wstage2[:, 0:D], reads=[b_ws2], writes=[b_bg])
        for h in range(H):
            for c in range(WCH):
                kb.dma(sy, wstage2[:, 0:128], bwin_d[h, c], writes=[b_ws2])
                kb.op(act, "activation", ebw[:, (h * WCH + c) * 128:(h * WCH + c + 1) * 128],
                      wstage2[:, 0:128], AF.Exp, reads=[b_ws2], writes=[b_ebw])
        kb.dma(sy, wstage2[:, 0:H * ND], bdyn_d, writes=[b_ws2])
        kb.op(act, "activation", ebd[:], wstage2[:, 0:H * ND], AF.Exp,
              reads=[b_ws2], writes=[b_ebd])

        for s in range(n_seq):
            for it in range(nt):
                tp = it % 2
                row = slice(it * 128, (it + 1) * 128)
                ptt, bpt = part2p[tp], b_part2p[tp]
                xnbt, bxnbt = xnb2p[tp], b_xnb2p[tp]
                xfm, bxfm = xnbfm2[tp], b_xnbfm2[tp]
                qtm, bqtm = q_tm[tp], b_qtm[tp]
                qfm, bqfm = q_fm[tp], b_qfm[tp]
                kb.dma(sy, ptt[:], part_d[s, row, :], reads=[b_partd], writes=[bpt])
                kb.dma(sy, xnbt[:], xnb_d[s, row, :], reads=[b_xnbd], writes=[bxnbt])
                kb.dma(act, xfm[:].rearrange("p (c q) -> p c q", c=8),
                       xnbt[:].rearrange("p (c q) -> p c q", c=8),
                       transpose=True, reads=[bxnbt], writes=[bxfm])
                for g in range(6):
                    ps_g, b_ps = qkv_slots[g % 3]
                    for dc in range(8):
                        kb.op(pe, "matmul", ps_g, xfm[:, dc * 128:(dc + 1) * 128],
                              wqkvT[:, dc * 3 * D + g * 512: dc * 3 * D + (g + 1) * 512],
                              start=(dc == 0), stop=(dc == 7),
                              reads=[bxfm, b_wq], writes=[b_ps])
                    if g < 2:
                        dstap, dbuf = qtm[:, g * 512:(g + 1) * 512], bqtm
                    elif g < 4:
                        dstap = k_tm[:, it * D + (g - 2) * 512: it * D + (g - 1) * 512]
                        dbuf = b_ktm[it]
                    else:
                        dstap = v_tm[:, it * D + (g - 4) * 512: it * D + (g - 3) * 512]
                        dbuf = b_vtm[it]
                    kb.op(dve, "scalar_tensor_tensor", dstap, ps_g, 1.0,
                          bias_qkv[:, g * 512:(g + 1) * 512], ALU.mult, ALU.add,
                          reads=[b_ps, b_bq], writes=[dbuf])
                kb.dma(act, qfm[:].rearrange("p (c q) -> p c q", c=8),
                       qtm[:].rearrange("p (c q) -> p c q", c=8),
                       transpose=True, reads=[bqtm], writes=[bqfm])
                kb.dma(act, kfm[:, (it % 3) * D:(it % 3 + 1) * D].rearrange("p (c q) -> p c q", c=8),
                       k_tm[:, it * D:(it + 1) * D].rearrange("p (c q) -> p c q", c=8),
                       transpose=True, reads=[b_ktm[it]], writes=[b_kfm[it % 3]])
                kb.op(dve, "memset", sdyn[:], NEG, writes=[b_sdyn])
                for mi, o in enumerate(DYADIC):
                    ot = o // 128
                    if it < ot or no_dyadic:
                        continue
                    kb.op(dve, "tensor_mul", g1[:], qtm[:],
                          k_tm[:, (it - ot) * D:(it - ot + 1) * D],
                          reads=[bqtm, b_ktm[it - ot]], writes=[b_g1])
                    kb.op(dve, "reduce_sum", sdyn[:, mi * H:(mi + 1) * H],
                          g1[:].rearrange("p (h d) -> p h d", d=HD), AX.X,
                          reads=[b_g1], writes=[b_sdyn])
                kb.op(act, "activation", edyn[:], sdyn[:], AF.Exp,
                      reads=[b_sdyn], writes=[b_edyn])
                kb.op(dve, "tensor_mul", edyn[:], edyn[:], ebd[:],
                      reads=[b_edyn, b_ebd], writes=[b_edyn])
                ev = bass.AP(edyn, 0, [[edyn[:].ap[0][0], 128], [1, H], [H, ND]])
                kb.op(dve, "reduce_sum", dsum[:], ev, AX.X,
                      reads=[b_edyn], writes=[b_dsum])
                for hg in range(4):
                    ep = hg % 2
                    heads = range(4 * hg, 4 * hg + 4)
                    for hh, h in sorted(enumerate(heads), key=lambda p: p[1] % 2):
                        for c in range(WCH):
                            jt = it - 2 + c
                            reg = P1[:, hh * 512 + c * 128: hh * 512 + (c + 1) * 128]
                            if jt < 0 or (no_c0 and c == 0):
                                kb.op(dve, "memset", reg, NEG, writes=[b_P1b[hh]])
                                continue
                            pb = (h % 2) * 64
                            lhsT = kfm[pb:pb + 64,
                                       (jt % 3) * D + (h // 2) * 128:(jt % 3) * D + (h // 2) * 128 + 128]
                            rhs = qfm[pb:pb + 64, (h // 2) * 128:(h // 2) * 128 + 128]
                            kb.op(pe, "matmul", reg, lhsT, rhs, start=True, stop=True,
                                  reads=[b_kfm[jt % 3], bqfm], writes=[b_P1b[hh]])
                    pv = P1[:].rearrange("p (a r) -> p a r", r=512)[:, :, 0:WCH * 128]
                    ev4 = e4[ep][:].rearrange("p (a r) -> p a r", r=WCH * 128)
                    kb.op(act, "activation", ev4, pv, AF.Exp,
                          reads=b_P1b, writes=[b_e4[ep]])
                    ebv = ebw[:, (4 * hg) * WCH * 128:(4 * hg + 4) * WCH * 128]
                    kb.op(dve, "tensor_mul", e4[ep][:], e4[ep][:], ebv,
                          reads=[b_e4[ep], b_ebw], writes=[b_e4[ep]])
                    for hh, h in enumerate(heads):
                        valid = [c for c in range(WCH)
                                 if it - 2 + c >= 0 and not (no_c0 and c == 0)]
                        for vi, c in enumerate(valid):
                            jt = it - 2 + c
                            ech = e4[ep][:, hh * 384 + c * 128: hh * 384 + (c + 1) * 128]
                            kb.op(pe, "matmul", P3[:, h:h + 1], ech, ones_col[:],
                                  start=(vi == 0), stop=(vi == len(valid) - 1),
                                  skip_group_check=True,
                                  reads=[b_e4[ep], b_ones], writes=[b_P3])
                            kb.op(pe, "matmul", P2[:, h * HD:(h + 1) * HD], ech,
                                  v_tm[:, jt * D + h * HD: jt * D + (h + 1) * HD],
                                  start=(vi == 0), stop=(vi == len(valid) - 1),
                                  skip_group_check=True,
                                  reads=[b_e4[ep], b_vtm[jt]], writes=[b_P2])
                kb.op(dve, "tensor_add", ssum[:], P3[:, 0:H], dsum[:],
                      reads=[b_P3, b_dsum], writes=[b_ssum])
                kb.op(dve, "reciprocal", srec[tp][:], ssum[:],
                      reads=[b_ssum], writes=[b_srec[tp]])
                vd = [] if no_dyadic else [o for o in DYADIC if it >= o // 128]
                if not vd:
                    kb.op(dve, "memset", g2[tp][:], 0.0, writes=[b_g2[tp]])
                for vi, o in enumerate(vd):
                    mi = DYADIC.index(o)
                    e_b = edyn[:, mi * H:(mi + 1) * H]
                    e_bc = e_b.unsqueeze(2).broadcast_to([128, H, HD])
                    tgt, tb = (g2[tp], b_g2[tp]) if vi == 0 else (g1, b_g1)
                    kb.op(dve, "tensor_tensor",
                          tgt[:].rearrange("p (h d) -> p h d", d=HD),
                          v_tm[:, (it - o // 128) * D:(it - o // 128 + 1) * D]
                          .rearrange("p (h d) -> p h d", d=HD),
                          e_bc, ALU.mult,
                          reads=[b_vtm[it - o // 128], b_edyn], writes=[tb])
                    if vi > 0:
                        kb.op(dve, "tensor_add", g2[tp][:], g2[tp][:], g1[:],
                              reads=[b_g2[tp], b_g1], writes=[b_g2[tp]])
                srec_bc = srec[tp][:].unsqueeze(2).broadcast_to([128, H, HD])
                v3h = lambda ap: ap.rearrange("p (h d) -> p h d", d=HD)
                kb.op(dve, "tensor_tensor", v3h(flat_sb[tp][:]), v3h(P2[:]), srec_bc,
                      ALU.mult, reads=[b_P2, b_srec[tp]], writes=[b_flat[tp]])
                kb.op(dve, "tensor_tensor", v3h(g2[tp][:]), v3h(g2[tp][:]), srec_bc,
                      ALU.mult, reads=[b_g2[tp], b_srec[tp]], writes=[b_g2[tp]])
                kb.op(dve, "tensor_add", flat_sb[tp][:], flat_sb[tp][:], g2[tp][:],
                      reads=[b_flat[tp], b_g2[tp]], writes=[b_flat[tp]])
                for g in range(2):
                    for dc in range(8):
                        kb.op(pe, "matmul", P2[:, g * 512:(g + 1) * 512],
                              xfm[:, dc * 128:(dc + 1) * 128],
                              wgT[:, dc * D + g * 512: dc * D + (g + 1) * 512],
                              start=(dc == 0), stop=(dc == 7),
                              reads=[bxfm, b_wg2], writes=[b_P2])
                kb.op(dve, "tensor_add", g4[tp][:], P2[:, 0:D], bias_gate[:],
                      reads=[b_P2, b_bg], writes=[b_g4[tp]])
                kb.op(act, "activation", g4[tp][:], g4[tp][:], AF.Tanh, scale=0.5,
                      reads=[b_g4[tp]], writes=[b_g4[tp]])
                kb.op(dve, "tensor_scalar", g4[tp][:], g4[tp][:], 0.5, 0.5,
                      ALU.mult, ALU.add, reads=[b_g4[tp]], writes=[b_g4[tp]])
                kb.op(dve, "tensor_mul", fg[tp][:], flat_sb[tp][:], g4[tp][:],
                      reads=[b_flat[tp], b_g4[tp]], writes=[b_fg[tp]])
                kb.dma(act, fg_fm[tp][:].rearrange("p (c q) -> p c q", c=8),
                       fg[tp][:].rearrange("p (c q) -> p c q", c=8),
                       transpose=True, reads=[b_fg[tp]], writes=[b_fgfm[tp]])
                for g in range(2):
                    for dc in range(8):
                        kb.op(pe, "matmul", P2[:, g * 512:(g + 1) * 512],
                              fg_fm[tp][:, dc * 128:(dc + 1) * 128],
                              woT[:, dc * D + g * 512: dc * D + (g + 1) * 512],
                              start=(dc == 0), stop=(dc == 7),
                              reads=[b_fgfm[tp], b_wo], writes=[b_P2])
                kb.op(dve, "tensor_add", ptt[:], P2[:], ptt[:],
                      reads=[b_P2, bpt], writes=[bpt])
                kb.dma(sy, out_d[s, row, :], ptt[:], reads=[bpt])


_NC_CACHE = {}


def kernel(**inputs):
    maps = core_inputs(inputs, n_cores=8)
    if "nc" not in _NC_CACHE:
        _NC_CACHE["nc"] = build(n_seq=2, nt=16, only_pass1=False)
    nc = _NC_CACHE["nc"]
    res = run_bass_kernel_spmd(nc, maps, core_ids=list(range(8)))
    out = np.concatenate([np.asarray(r["out"]) for r in res.results], axis=0)
    return out.astype(np.float32)


# version 03
# speedup vs baseline: 2.1357x; 2.0712x over previous
import numpy as np
from contextlib import ExitStack

import concourse.bass as bass
import concourse.mybir as mybir
from concourse.bass_utils import run_bass_kernel_spmd

"""Host-side input prep shared by dev scripts and kernel.py."""

_DENSE_LOCAL_W = 64
_DYADIC_ALL = [96, 128, 192, 256, 384, 512, 768, 1024, 1536]
OFFSETS = sorted(set(range(0, _DENSE_LOCAL_W + 1)) | set(_DYADIC_ALL))
WIN_OFF = set(range(0, 65)) | {96, 128, 192}
DYADIC = [256, 384, 512, 768, 1024, 1536]
H = 16
NEG = -1e30


def make_bwin(pos_bias):
    pb = {o: pos_bias[m] for m, o in enumerate(OFFSETS)}
    out = np.full((H, 3, 128, 128), NEG, dtype=np.float32)
    for c in range(3):
        base = (2 - c) * 128
        for j in range(128):
            for i in range(128):
                off = base + i - j
                if off in WIN_OFF:
                    out[:, c, j, i] = pb[off]
    return out


def make_bdyn(pos_bias):
    idx = {o: m for m, o in enumerate(OFFSETS)}
    v = np.empty((len(DYADIC), H), dtype=np.float32)
    for mi, o in enumerate(DYADIC):
        v[mi, :] = pos_bias[idx[o]]
    row = v.reshape(-1)
    return np.broadcast_to(row, (128, H * len(DYADIC))).copy()


def core_inputs(ins, n_cores=8):
    x = np.ascontiguousarray(ins["L0_attn_x"], dtype=np.float32)
    bwin = np.ascontiguousarray(make_bwin(np.asarray(ins["L0_attn_pos_bias"])))
    bdyn = np.ascontiguousarray(make_bdyn(np.asarray(ins["L0_attn_pos_bias"])))
    com = {
        "wg": np.ascontiguousarray(ins["L0_ffn_w_gate"], dtype=np.float32),
        "wu": np.ascontiguousarray(ins["L0_ffn_w_up"], dtype=np.float32),
        "wd": np.ascontiguousarray(ins["L0_ffn_w_down"], dtype=np.float32),
        "wqkv": np.ascontiguousarray(ins["L0_attn_Wqkv"], dtype=np.float32),
        "bqkvb": np.ascontiguousarray(
            np.broadcast_to(ins["L0_attn_bqkv"], (128, 3072))).astype(np.float32),
        "wgate": np.ascontiguousarray(ins["L0_attn_Wgate"], dtype=np.float32),
        "bgateb": np.ascontiguousarray(
            np.broadcast_to(ins["L0_attn_bgate"], (128, 1024))).astype(np.float32),
        "wout": np.ascontiguousarray(ins["L0_attn_Wout"], dtype=np.float32),
        "boutb": np.ascontiguousarray(
            np.broadcast_to(ins["L0_attn_bout"], (128, 1024))).astype(np.float32),
        "bwin": bwin,
        "bdyn": bdyn,
    }
    per = x.shape[0] // n_cores
    maps = []
    for c in range(n_cores):
        m = dict(com)
        m["x"] = np.ascontiguousarray(x[c * per:(c + 1) * per])
        maps.append(m)
    return maps


"""Kernel-builder helpers: per-engine semaphore bookkeeping + dependency tracking.

Compute instructions increment their engine's progress semaphore (+1 each).
DMA queues use R rotating semaphores with issue-gating: a DMA on rotation
slot j may only issue once the previous DMA on slot j completed, so each
slot's semaphore is serialized and every 16*m boundary is a deterministic
wait value (concurrent DMAs on ONE semaphore post their 16 increments
interleaved, making intermediate waits racy).
"""

NDMASEM = 6


class Buf:
    __slots__ = ("name", "writer", "readers")

    def __init__(self, name):
        self.name = name
        self.writer = None
        self.readers = []


class Eng:
    def __init__(self, kb, name, eng, sem, dma_sems):
        self.kb = kb
        self.name = name
        self.e = eng
        self.sem = sem
        self.dma_sems = dma_sems
        self.dma_counts = [0] * len(dma_sems)
        self.dma_idx = 0
        self.n = 0
        self.known = {}

    def _wait(self, token):
        sem, val = token
        if self.name == "pe" and sem is self.sem:
            return
        if self.known.get(id(sem), -1) >= val:
            return
        self.e.wait_ge(sem, val)
        self.known[id(sem)] = val


class KB:
    def __init__(self, nc, es):
        self.nc = nc
        mk = lambda nm: es.enter_context(nc.semaphore(nm))
        self.pe = Eng(self, "pe", nc.tensor, mk("sem_pe"), [])
        self.dve = Eng(self, "dve", nc.vector, mk("sem_dve"), [])
        self.act = Eng(self, "act", nc.scalar, mk("sem_act"),
                       [mk(f"dsc{i}") for i in range(NDMASEM)])
        self.pool = Eng(self, "pool", nc.gpsimd, mk("sem_pool"),
                        [mk(f"dg{i}") for i in range(NDMASEM)])
        self.sy = Eng(self, "sy", nc.sync, mk("sem_sy"),
                      [mk(f"dsy{i}") for i in range(NDMASEM)])
        self.engines = [self.pe, self.dve, self.act, self.pool, self.sy]

    def op(self, eng, method, *args, reads=(), writes=(), **kwargs):
        for b in reads:
            if b.writer is not None:
                eng._wait(b.writer)
        for b in writes:
            if b.writer is not None:
                eng._wait(b.writer)
            for t in b.readers:
                eng._wait(t)
        inst = getattr(eng.e, method)(*args, **kwargs)
        eng.n += 1
        inst.then_inc(eng.sem, 1)
        tok = (eng.sem, eng.n)
        for b in reads:
            b.readers.append(tok)
        for b in writes:
            b.writer = tok
            b.readers = []
        return tok

    def dma(self, eng, dst, src, reads=(), writes=(), transpose=False, **kw):
        for b in reads:
            if b.writer is not None:
                eng._wait(b.writer)
        for b in writes:
            if b.writer is not None:
                eng._wait(b.writer)
            for t in b.readers:
                eng._wait(t)
        j = eng.dma_idx % len(eng.dma_sems)
        eng.dma_idx += 1
        sem = eng.dma_sems[j]
        if eng.dma_counts[j] > 0:
            eng._wait((sem, 16 * eng.dma_counts[j]))
        if transpose:
            inst = eng.e.dma_start_transpose(dst, src)
        else:
            inst = eng.e.dma_start(dst, src, **kw)
        eng.dma_counts[j] += 1
        inst.then_inc(sem, 16)
        tok = (sem, 16 * eng.dma_counts[j])
        for b in reads:
            b.readers.append(tok)
        for b in writes:
            b.writer = tok
            b.readers = []
        return tok

    def barrier_token(self, eng):
        return (eng.sem, eng.n)

    def finalize(self):
        for e in self.engines:
            for j, sem in enumerate(e.dma_sems):
                if e.dma_counts[j] > 0:
                    e._wait((sem, 16 * e.dma_counts[j]))


def full_barrier(kb):
    toks = []
    for e in kb.engines:
        if e.n > 0:
            toks.append((e.sem, e.n))
        for j, sem in enumerate(e.dma_sems):
            if e.dma_counts[j] > 0:
                toks.append((sem, 16 * e.dma_counts[j]))
    for e in kb.engines:
        for t in toks:
            e._wait(t)


"""Megakernel builder for mk_prb_d1: out = x + attn(LN(x)) + ffn(LN(x)).

Per core: n_seq sequences of 2048 tokens. Two weight-phased passes inside ONE
program: pass1 = FFN (fp4 SwiGLU) writing partial = x + ffn + bout to DRAM
scratch (plus xn bf16 cache); pass2 = sparse attention reading partial/xn and
writing out.
"""

F32 = mybir.dt.float32
BF16 = mybir.dt.bfloat16
F16 = mybir.dt.float16
ALU = mybir.AluOpType
AX = mybir.AxisListType
AF = mybir.ActivationFunctionType

D = 1024
N = 2048
I = 2816
ICHUNKS = [(0, 512), (512, 512), (1024, 512), (1536, 512), (2048, 512), (2560, 256)]
H = 16
HD = 64
WCH = 3
DYADIC = [256, 384, 512, 768, 1024, 1536]
ND = len(DYADIC)
EPS = 1e-5
QW = 1408


def emit_ln(kb, t, b, x_ap, xnb_ap, bx, bxnb):
    dve, act = kb.dve, kb.act
    kb.op(dve, "reduce_sum", t["s1"][:], x_ap, AX.X, reads=[bx], writes=[b["s1"]])
    kb.op(dve, "tensor_mul", t["xsq"][:], x_ap, x_ap, reads=[bx], writes=[b["xsq"]])
    kb.op(dve, "reduce_sum", t["s2"][:], t["xsq"][:], AX.X, reads=[b["xsq"]], writes=[b["s2"]])
    kb.op(dve, "tensor_scalar", t["mean"][:], t["s1"][:], 1.0 / D, None, ALU.mult,
          reads=[b["s1"]], writes=[b["mean"]])
    kb.op(dve, "tensor_mul", t["tmp1"][:], t["mean"][:], t["mean"][:],
          reads=[b["mean"]], writes=[b["tmp1"]])
    kb.op(dve, "scalar_tensor_tensor", t["var"][:], t["s2"][:], 1.0 / D, t["tmp1"][:],
          ALU.mult, ALU.subtract, reads=[b["s2"], b["tmp1"]], writes=[b["var"]])
    kb.op(dve, "tensor_scalar", t["var"][:], t["var"][:], 1.0, EPS, ALU.mult, ALU.add,
          reads=[b["var"]], writes=[b["var"]])
    kb.op(dve, "memset", t["y"][:], 1.0, writes=[b["y"]])
    for _ in range(4):
        kb.op(dve, "tensor_mul", t["tmp1"][:], t["y"][:], t["y"][:],
              reads=[b["y"]], writes=[b["tmp1"]])
        kb.op(dve, "tensor_mul", t["tmp1"][:], t["tmp1"][:], t["var"][:],
              reads=[b["tmp1"], b["var"]], writes=[b["tmp1"]])
        kb.op(dve, "tensor_scalar", t["tmp1"][:], t["tmp1"][:], -0.5, 1.5, ALU.mult, ALU.add,
              reads=[b["tmp1"]], writes=[b["tmp1"]])
        kb.op(dve, "tensor_mul", t["y"][:], t["y"][:], t["tmp1"][:],
              reads=[b["y"], b["tmp1"]], writes=[b["y"]])
    kb.op(dve, "scalar_tensor_tensor", t["nmr"][:], t["mean"][:], -1.0, t["y"][:],
          ALU.mult, ALU.mult, reads=[b["mean"], b["y"]], writes=[b["nmr"]])
    kb.op(act, "activation", xnb_ap, x_ap, AF.Identity, bias=t["nmr"][:], scale=t["y"][:],
          reads=[bx, b["nmr"], b["y"]], writes=[bxnb])


def emit_quant(kb, t, b, xs_ap, xq_ap, bxs, bxq, width):
    dve, act = kb.dve, kb.act
    nb = width // 16
    W = width
    I16 = mybir.dt.int16
    v3 = lambda ap: ap.rearrange("p (a c) -> p a c", c=16)
    xa, qa, qb_, qd = (t["qxa"][:, :W], t["qa"][:, :W], t["qb"][:, :W], t["qd"][:, :W])
    m, mc, r6f = t["qm"][:, :nb], t["qmc"][:, :nb], t["qr6f"][:, :nb]
    r6h, sbh = t["qr6h"][:, :nb], t["qsbh"][:, :nb]
    B = {k: b["q_" + k] for k in ("xa", "m", "mc", "r6f", "r6h", "sbh", "a", "b", "d")}
    kb.op(act, "activation", xa, xs_ap, AF.Abs, reads=[bxs], writes=[B["xa"]])
    kb.op(dve, "reduce_max", m, v3(xa), AX.X, reads=[B["xa"]], writes=[B["m"]])
    kb.op(dve, "tensor_scalar", mc, m, 1e-4, None, ALU.max, reads=[B["m"]], writes=[B["mc"]])
    kb.op(dve, "tensor_scalar", sbh, mc, 1.0 / 6.0, None, ALU.mult,
          reads=[B["mc"]], writes=[B["sbh"]])
    kb.op(dve, "reciprocal", r6f, mc, reads=[B["mc"]], writes=[B["r6f"]])
    kb.op(dve, "tensor_scalar", r6h, r6f, 6.0, None, ALU.mult,
          reads=[B["r6f"]], writes=[B["r6h"]])
    bc = r6h.unsqueeze(2).broadcast_to([128, nb, 16])
    kb.op(dve, "tensor_tensor", v3(xa), v3(xa), bc, ALU.mult,
          reads=[B["xa"], B["r6h"]], writes=[B["xa"]])
    u = xa
    kb.op(dve, "tensor_scalar", qa.bitcast(I16), u, 2.0, None, ALU.mult,
          reads=[B["xa"]], writes=[B["a"]])
    kb.op(dve, "tensor_scalar", qa, qa.bitcast(I16), 4.0, 0.5, ALU.min, ALU.mult,
          reads=[B["a"]], writes=[B["a"]])
    kb.op(dve, "tensor_scalar", qb_.bitcast(I16), u, 1.0, None, ALU.mult,
          reads=[B["xa"]], writes=[B["b"]])
    kb.op(dve, "tensor_scalar", qb_, qb_.bitcast(I16), 2.0, 4.0, ALU.max, ALU.min,
          reads=[B["b"]], writes=[B["b"]])
    kb.op(dve, "tensor_scalar", qd.bitcast(I16), u, 1000.0, -5000.0, ALU.mult, ALU.add,
          reads=[B["xa"]], writes=[B["d"]])
    kb.op(dve, "tensor_scalar", qd, qd.bitcast(I16), 0.0, 1.0, ALU.max, ALU.min,
          reads=[B["d"]], writes=[B["d"]])
    kb.op(dve, "scalar_tensor_tensor", qa, qa, 2.0, qb_, ALU.subtract, ALU.add,
          reads=[B["a"], B["b"]], writes=[B["a"]])
    kb.op(dve, "scalar_tensor_tensor", qa, qd, 2.0, qa, ALU.mult, ALU.add,
          reads=[B["d"], B["a"]], writes=[B["a"]])
    kb.op(act, "activation", qd, xs_ap, AF.Sign, reads=[bxs], writes=[B["d"]])
    kb.op(dve, "scalar_tensor_tensor", qa, qa, 1.0, qd, ALU.mult, ALU.mult,
          reads=[B["a"], B["d"]], writes=[B["a"]])
    sb_bc = sbh.unsqueeze(2).broadcast_to([128, nb, 16])
    kb.op(dve, "tensor_tensor", v3(xq_ap), v3(qa), sb_bc, ALU.mult,
          reads=[B["a"], B["sbh"]], writes=[bxq])


def build(n_seq=2, nt=16, only_pass1=False, dbg=None, no_dyadic=False, no_c0=False):
    nc = bass.Bass("TRN2", target_bir_lowering=False)
    es = ExitStack()
    kb = KB(nc, es)
    sbt = lambda name, shape, dt=BF16: es.enter_context(nc.sbuf_tensor(name, shape, dt))

    x_d = nc.dram_tensor("x", [n_seq, N, D], F32, kind="ExternalInput").ap()
    out_d = nc.dram_tensor("out", [n_seq, N, D], F32, kind="ExternalOutput").ap()
    part_d = nc.dram_tensor("part", [n_seq, N, D], F32, kind="Internal").ap()
    xnb_d = nc.dram_tensor("xnbc", [n_seq, N, D], BF16, kind="Internal").ap()
    wg_d = nc.dram_tensor("wg", [I, D], F32, kind="ExternalInput").ap()
    wu_d = nc.dram_tensor("wu", [I, D], F32, kind="ExternalInput").ap()
    wd_d = nc.dram_tensor("wd", [D, I], F32, kind="ExternalInput").ap()
    wqkv_d = nc.dram_tensor("wqkv", [3 * D, D], F32, kind="ExternalInput").ap()
    bqkv_d = nc.dram_tensor("bqkvb", [128, 3 * D], F32, kind="ExternalInput").ap()
    wgt_d = nc.dram_tensor("wgate", [D, D], F32, kind="ExternalInput").ap()
    bgt_d = nc.dram_tensor("bgateb", [128, D], F32, kind="ExternalInput").ap()
    wo_d = nc.dram_tensor("wout", [D, D], F32, kind="ExternalInput").ap()
    bo_d = nc.dram_tensor("boutb", [128, D], F32, kind="ExternalInput").ap()
    bwin_d = nc.dram_tensor("bwin", [H, WCH, 128, 128], F32, kind="ExternalInput").ap()
    bdyn_d = nc.dram_tensor("bdyn", [128, H * ND], F32, kind="ExternalInput").ap()

    B = {}
    def buf(nm):
        if nm not in B:
            B[nm] = Buf(nm)
        return B[nm]

    xnb = sbt("xnb", [128, D])
    small = {nm: sbt("ln_" + nm, [128, 1], F32)
             for nm in ("s1", "s2", "mean", "var", "y", "nmr", "tmp1")}
    part_t = sbt("part_t", [128, D], F32)
    ln_b = {nm: buf(nm) for nm in ("s1", "s2", "mean", "var", "y", "nmr", "tmp1", "xsq")}
    for nm in ("xa", "m", "mc", "r6f", "r6h", "sbh", "a", "b", "d"):
        buf("q_" + nm)
    b_xnb, b_part = buf("xnb"), buf("part_t")

    pe, dve, act, pool, sy = kb.pe, kb.dve, kb.act, kb.pool, kb.sy

    with ExitStack() as es1:
        s1t = lambda name, shape, dt=BF16: es1.enter_context(nc.sbuf_tensor(name, shape, dt))
        ffnw = s1t("ffnw", [128, 3 * 8 * I])
        WGO, WUO, WDO = 0, 8 * I, 16 * I
        FST = 3 * 8 * I
        wgqT = ffnw[:, WGO:WGO + 8 * I]
        wuqT = ffnw[:, WUO:WUO + 8 * I]
        wdqT = ffnw[:, WDO:WDO + 22 * D]
        b_wgqT, b_wuqT, b_wdqT = buf("wgqT"), buf("wuqT"), buf("wdqT")
        x_t = [s1t(f"x_t{p}", [128, D], F32) for p in range(2)]
        xnf1 = s1t("xnf", [128, D], F32)
        xnf = [xnf1, xnf1]
        xsq = xnf1
        ln_t = dict(small)
        b_x = [buf(f"x_t{p}") for p in range(2)]
        b_xnf1 = buf("xnf")
        b_xnf = [b_xnf1, b_xnf1]
        ln_b["xsq"] = b_xnf1
        q_t = {}
        for nm in ("qxa", "qa", "qb", "qd"):
            q_t[nm] = s1t(nm, [128, QW], F16)
        for nm in ("qm", "qmc", "qr6f"):
            q_t[nm] = s1t(nm, [128, QW // 16], F32)
        for nm in ("qr6h", "qsbh"):
            q_t[nm] = s1t(nm, [128, QW // 16], F16)
        xq = [s1t(f"xq{p}", [128, D]) for p in range(2)]
        xq_fm = [s1t(f"xq_fm{p}", [128, D]) for p in range(2)]
        hq = [s1t(f"hq{p}", [128, I]) for p in range(2)]
        hq_fm = [s1t(f"hq_fm{p}", [128, I]) for p in range(2)]
        h_sg = s1t("h_sg", [128, 512], F16)
        h_gt = s1t("h_gt", [128, 512], F16)
        h_up = s1t("h_up", [128, 512], F16)
        bias_out = s1t("bias_out", [128, D])
        b_xq = [buf(f"xq{p}") for p in range(2)]
        b_xqfm = [buf(f"xq_fm{p}") for p in range(2)]
        b_hq = [buf(f"hq{p}") for p in range(2)]
        b_hqfm = [buf(f"hq_fm{p}") for p in range(2)]
        b_hsg, b_hgt, b_hup = buf("h_sg"), buf("h_gt"), buf("h_up")
        b_bo = buf("bias_out")
        xnbp = [xnb, xnb]
        b_xnbp = [b_xnb, b_xnb]
        part_p = [part_t, s1t("part2", [128, D], F32)]
        b_partp = [b_part, buf("part2")]
        wstage = x_t[0]
        b_ws = b_x[0]
        kb.dma(sy, xnf1[:], bo_d, writes=[b_xnf1])
        kb.op(act, "copy", bias_out[:], xnf1[:], reads=[b_xnf1], writes=[b_bo])
        for w_d, wTO, bwT, ntile, tw, ost in (
            (wg_d, WGO, b_wgqT, 22, D, I), (wu_d, WUO, b_wuqT, 22, D, I),
            (wd_d, WDO, b_wdqT, 8, I, D),
        ):
            for it in range(ntile):
                wst, bst = x_t[it % 2], b_x[it % 2]
                hqt, bhqt = hq[it % 2], b_hq[it % 2]
                for poff in range(0, tw, D):
                    cw = min(D, tw - poff)
                    kb.dma(sy, wst[:, 0:cw], w_d[it * 128:(it + 1) * 128, poff:poff + cw],
                           writes=[bst])
                    emit_quant(kb, q_t, B, wst[:, 0:cw], hqt[:, poff:poff + cw],
                               bst, bhqt, cw)
                nblk = tw // 128
                dst = bass.AP(ffnw, wTO + it * 128,
                              [[FST, 128], [ost, nblk], [1, 128]])
                src = hqt[:, 0:tw].rearrange("p (c q) -> p c q", c=nblk)
                kb.dma(act, dst, src, transpose=True, reads=[bhqt], writes=[bwT])
        pg = [es1.enter_context(nc.psum_tensor(f"pg{p}", [128, 512], F32)) for p in range(2)]
        pu = [es1.enter_context(nc.psum_tensor(f"pu{p}", [128, 512], F32)) for p in range(2)]
        pd = [es1.enter_context(nc.psum_tensor(f"pd{p}", [128, D], F32)) for p in range(2)]
        b_pg = [buf(f"pg{p}") for p in range(2)]
        b_pu = [buf(f"pu{p}") for p in range(2)]
        b_pd = [buf(f"pd{p}") for p in range(2)]

        for s in range(n_seq):
            for it in range(nt):
                tp = it % 2
                row = slice(it * 128, (it + 1) * 128)
                xt, bx = x_t[tp], b_x[tp]
                xnfp, bxnf = xnf[tp], b_xnf[tp]
                xnbt, bxnbt = xnbp[tp], b_xnbp[tp]
                xqt, bxq = xq[tp], b_xq[tp]
                xqf, bxqf = xq_fm[tp], b_xqfm[tp]
                hqt, bhq = hq[tp], b_hq[tp]
                hqf, bhqf = hq_fm[tp], b_hqfm[tp]
                hfl_ap = hqf[:].bitcast(F16)
                ptt, bpt = part_p[tp], b_partp[tp]
                kb.dma(sy, xt[:], x_d[s, row, :], writes=[bx])
                ln_t["xsq"] = xnfp
                emit_ln(kb, ln_t, ln_b, xt[:], xnbt[:], bx, bxnbt)
                kb.dma(sy, xnb_d[s, row, :], xnbt[:], reads=[bxnbt], writes=[buf("xnb_d")])
                kb.op(kb.act, "activation", xnfp[:], xt[:], AF.Identity,
                      bias=small["nmr"][:], scale=small["y"][:],
                      reads=[bx, ln_b["nmr"], ln_b["y"]], writes=[bxnf])
                emit_quant(kb, q_t, B, xnfp[:], xqt[:], bxnf, bxq, D)
                kb.dma(act, xqf[:].rearrange("p (c q) -> p c q", c=8),
                       xqt[:].rearrange("p (c q) -> p c q", c=8),
                       transpose=True, reads=[bxq], writes=[bxqf])
                for ci, (off, cw) in enumerate(ICHUNKS):
                    cp = ci % 2
                    for dc in range(8):
                        lhsT = xqf[:, dc * 128:(dc + 1) * 128]
                        kb.op(pe, "matmul", pg[cp][:, 0:cw], lhsT,
                              wgqT[:, dc * I + off: dc * I + off + cw],
                              start=(dc == 0), stop=(dc == 7),
                              reads=[bxqf, b_wgqT], writes=[b_pg[cp]])
                        kb.op(pe, "matmul", pu[cp][:, 0:cw], lhsT,
                              wuqT[:, dc * I + off: dc * I + off + cw],
                              start=(dc == 0), stop=(dc == 7),
                              reads=[bxqf, b_wuqT], writes=[b_pu[cp]])
                    kb.op(act, "activation", h_sg[:, 0:cw], pg[cp][:, 0:cw], AF.Tanh,
                          scale=0.5, reads=[b_pg[cp]], writes=[b_hsg])
                    kb.op(act, "copy", h_gt[:, 0:cw], pg[cp][:, 0:cw],
                          reads=[b_pg[cp]], writes=[b_hgt])
                    kb.op(act, "copy", h_up[:, 0:cw], pu[cp][:, 0:cw],
                          reads=[b_pu[cp]], writes=[b_hup])
                    kb.op(dve, "tensor_scalar", h_sg[:, 0:cw], h_sg[:, 0:cw], 0.5, 0.5,
                          ALU.mult, ALU.add, reads=[b_hsg], writes=[b_hsg])
                    kb.op(dve, "tensor_mul", h_gt[:, 0:cw], h_gt[:, 0:cw], h_up[:, 0:cw],
                          reads=[b_hgt, b_hup], writes=[b_hgt])
                    kb.op(dve, "tensor_mul", hfl_ap[:, off:off + cw], h_gt[:, 0:cw],
                          h_sg[:, 0:cw], reads=[b_hgt, b_hsg], writes=[bhqf])
                emit_quant(kb, q_t, B, hfl_ap[:, 0:QW], hqt[:, 0:QW], bhqf, bhq, QW)
                emit_quant(kb, q_t, B, hfl_ap[:, QW:I], hqt[:, QW:I], bhqf, bhq, I - QW)
                kb.dma(act, hqf[:].rearrange("p (c q) -> p c q", c=22),
                       hqt[:].rearrange("p (c q) -> p c q", c=22),
                       transpose=True, reads=[bhq], writes=[bhqf])
                for half in range(2):
                    for ic in range(22):
                        kb.op(pe, "matmul", pd[tp][:, half * 512:(half + 1) * 512],
                              hqf[:, ic * 128:(ic + 1) * 128],
                              wdqT[:, ic * D + half * 512: ic * D + (half + 1) * 512],
                              start=(ic == 0), stop=(ic == 21),
                              reads=[bhqf, b_wdqT], writes=[b_pd[tp]])
                kb.op(dve, "tensor_add", ptt[:], pd[tp][:], xt[:],
                      reads=[b_pd[tp], bx], writes=[bpt])
                kb.op(dve, "tensor_add", ptt[:], ptt[:], bias_out[:],
                      reads=[bpt, b_bo], writes=[bpt])
                dst = (out_d if only_pass1 else part_d)[s, row, :]
                kb.dma(sy, dst, ptt[:], reads=[bpt],
                       writes=([] if only_pass1 else [buf("part_d")]))

    if only_pass1:
        kb.finalize()
        es.close()
        return nc

    env = dict(locals()); env['sbt'] = sbt; env['dbg'] = dbg
    env['no_dyadic'] = no_dyadic; env['no_c0'] = no_c0
    build_pass2(nc, es, kb, B, buf, env)
    kb.finalize()
    es.close()
    return nc


def build_pass2(nc, es, kb, B, buf, env):
    sbt = env["sbt"]
    small = env["small"]
    xnb, part_t = env["xnb"], env["part_t"]
    x_d, out_d, part_d, xnb_d = env["x_d"], env["out_d"], env["part_d"], env["xnb_d"]
    wqkv_d, bqkv_d = env["wqkv_d"], env["bqkv_d"]
    wgt_d, bgt_d, wo_d = env["wgt_d"], env["bgt_d"], env["wo_d"]
    bwin_d, bdyn_d = env["bwin_d"], env["bdyn_d"]
    n_seq, nt = env["n_seq"], env["nt"]
    no_dyadic = env.get("no_dyadic"); no_c0 = env.get("no_c0")
    pe, dve, act, pool, sy = kb.pe, kb.dve, kb.act, kb.pool, kb.sy
    b_partd, b_xnbd = buf("part_d"), buf("xnb_d")
    NEG = -1e30

    with ExitStack() as es2:
        s2t = lambda name, shape, dt=BF16: es2.enter_context(nc.sbuf_tensor(name, shape, dt))
        attnw = s2t("attnw", [128, 8 * 3 * D + 16 * D])
        ATST = 8 * 3 * D + 16 * D
        WQO, WGO2, WOO = 0, 8 * 3 * D, 8 * 3 * D + 8 * D
        wqkvT = attnw[:, WQO:WQO + 24 * D]
        wgT = attnw[:, WGO2:WGO2 + 8 * D]
        woT = attnw[:, WOO:WOO + 8 * D]
        b_wq, b_wg2, b_wo = buf("wqkvT"), buf("wgT"), buf("woT")
        k_tm = s2t("k_tm", [128, 16 * D])
        v_tm = s2t("v_tm", [128, 16 * D])
        kfm = s2t("kfm", [128, 3 * D])
        ebw = s2t("ebw", [128, H * WCH * 128])
        ebd = s2t("ebd", [128, H * ND])
        bias_qkv = s2t("bias_qkv", [128, 3 * D])
        bias_gate = s2t("bias_gate", [128, D])
        ones_col = s2t("ones_col", [128, 1])
        qtm1 = s2t("q_tm", [128, D])
        q_tm = [qtm1, qtm1]
        q_fm = [s2t(f"q_fm{p}", [128, D]) for p in range(2)]
        e4 = [s2t(f"e4{p}", [128, 4 * WCH * 128]) for p in range(2)]
        sdyn = s2t("sdyn", [128, H * ND], F32)
        edyn = s2t("edyn", [128, H * ND])
        dsum = s2t("dsum", [128, H], F32)
        ssum = s2t("ssum", [128, H], F32)
        srec = [s2t(f"srec{p}", [128, H], F32) for p in range(2)]
        g1 = s2t("g1", [128, D])
        g21 = s2t("g2", [128, D])
        g2 = [g21, g21]
        g41 = s2t("g4", [128, D])
        g4 = [g41, g41]
        flat_sb = [s2t(f"flat_sb{p}", [128, D]) for p in range(2)]
        fg1t = s2t("fg", [128, D])
        fg = [fg1t, fg1t]
        fgf1 = s2t("fg_fm", [128, D])
        fg_fm = [fgf1, fgf1]
        xnb2p = [xnb, xnb]
        xnbfm2 = [s2t(f"xnbfm2{p}", [128, D]) for p in range(2)]
        part2p = [part_t, part_t]
        b_part2p = [B["part_t"], B["part_t"]]
        wstage2 = part_t
        b_ws2 = B["part_t"]
        b_ktm = [buf(f"k_tm{j}") for j in range(16)]
        b_vtm = [buf(f"v_tm{j}") for j in range(16)]
        b_kfm = [buf(f"kfm{j}") for j in range(3)]
        b_ebw, b_ebd = buf("ebw"), buf("ebd")
        b_bq, b_bg = buf("bias_qkv"), buf("bias_gate")
        b_ones = buf("ones_col")
        b_qtm1 = buf("q_tm")
        b_qtm = [b_qtm1, b_qtm1]
        b_qfm = [buf(f"q_fm{p}") for p in range(2)]
        b_e4 = [buf(f"e4{p}") for p in range(2)]
        b_sdyn, b_edyn = buf("sdyn"), buf("edyn")
        b_dsum, b_ssum = buf("dsum"), buf("ssum")
        b_srec = [buf(f"srec{p}") for p in range(2)]
        b_g1 = buf("g1")
        b_g21 = buf("g2")
        b_g2 = [b_g21, b_g21]
        b_g41 = buf("g4")
        b_g4 = [b_g41, b_g41]
        b_flat = [buf(f"flat_sb{p}") for p in range(2)]
        b_fg1 = buf("fg")
        b_fg = [b_fg1, b_fg1]
        b_fgf1 = buf("fg_fm")
        b_fgfm = [b_fgf1, b_fgf1]
        b_xnb2p = [B["xnb"], B["xnb"]]
        b_xnbfm2 = [buf(f"xnbfm2{p}") for p in range(2)]

        P1 = es2.enter_context(nc.psum_tensor("P1", [128, 2048], F32))
        P2 = es2.enter_context(nc.psum_tensor("P2", [128, D], F32))
        P3 = es2.enter_context(nc.psum_tensor("P3", [128, 512], F32))
        P4 = es2.enter_context(nc.psum_tensor("P4", [128, 512], F32))
        b_P1b = [buf(f"P1b{b}") for b in range(4)]
        b_P2, b_P3, b_P4 = buf("P2"), buf("P3"), buf("P4")
        qkv_slots = [(P1[:, 0:512], b_P1b[0]), (P1[:, 512:1024], b_P1b[1]),
                     (P4[:, 0:512], b_P4)]

        kb.op(dve, "memset", ones_col[:], 1.0, writes=[b_ones])
        kb.op(dve, "memset", P1[:], 0.0, writes=b_P1b)
        kb.op(dve, "memset", P2[:], 0.0, writes=[b_P2])
        kb.op(dve, "memset", P3[:], 0.0, writes=[b_P3])
        kb.op(dve, "memset", P4[:], 0.0, writes=[b_P4])
        for it in range(24):
            kb.dma(sy, wstage2[:], wqkv_d[it * 128:(it + 1) * 128, :], writes=[b_ws2])
            scale = 0.125 if 8 <= it < 16 else 1.0
            kb.op(act, "activation", xnb[:], wstage2[:], AF.Identity, scale=scale,
                  reads=[b_ws2], writes=[B["xnb"]])
            dst = bass.AP(attnw, WQO + it * 128, [[ATST, 128], [3 * D, 8], [1, 128]])
            kb.dma(act, dst, xnb[:].rearrange("p (c q) -> p c q", c=8),
                   transpose=True, reads=[B["xnb"]], writes=[b_wq])
        for w_d, base, bw in ((wgt_d, WGO2, b_wg2), (wo_d, WOO, b_wo)):
            for it in range(8):
                kb.dma(sy, wstage2[:], w_d[it * 128:(it + 1) * 128, :], writes=[b_ws2])
                kb.op(act, "copy", xnb[:], wstage2[:], reads=[b_ws2], writes=[B["xnb"]])
                dst = bass.AP(attnw, base + it * 128, [[ATST, 128], [D, 8], [1, 128]])
                kb.dma(act, dst, xnb[:].rearrange("p (c q) -> p c q", c=8),
                       transpose=True, reads=[B["xnb"]], writes=[bw])
        for c in range(3):
            kb.dma(sy, wstage2[:, 0:D], bqkv_d[:, c * D:(c + 1) * D], writes=[b_ws2])
            kb.op(act, "activation", bias_qkv[:, c * D:(c + 1) * D], wstage2[:, 0:D],
                  AF.Identity, scale=(0.125 if c == 1 else 1.0),
                  reads=[b_ws2], writes=[b_bq])
        kb.dma(sy, wstage2[:, 0:D], bgt_d, writes=[b_ws2])
        kb.op(act, "copy", bias_gate[:], wstage2[:, 0:D], reads=[b_ws2], writes=[b_bg])
        for h in range(H):
            for c in range(WCH):
                kb.dma(sy, wstage2[:, 0:128], bwin_d[h, c], writes=[b_ws2])
                kb.op(act, "activation", ebw[:, (h * WCH + c) * 128:(h * WCH + c + 1) * 128],
                      wstage2[:, 0:128], AF.Exp, reads=[b_ws2], writes=[b_ebw])
        kb.dma(sy, wstage2[:, 0:H * ND], bdyn_d, writes=[b_ws2])
        kb.op(act, "activation", ebd[:], wstage2[:, 0:H * ND], AF.Exp,
              reads=[b_ws2], writes=[b_ebd])

        for s in range(n_seq):
            for it in range(nt):
                tp = it % 2
                row = slice(it * 128, (it + 1) * 128)
                ptt, bpt = part2p[tp], b_part2p[tp]
                xnbt, bxnbt = xnb2p[tp], b_xnb2p[tp]
                xfm, bxfm = xnbfm2[tp], b_xnbfm2[tp]
                qtm, bqtm = q_tm[tp], b_qtm[tp]
                qfm, bqfm = q_fm[tp], b_qfm[tp]
                kb.dma(sy, ptt[:], part_d[s, row, :], reads=[b_partd], writes=[bpt])
                kb.dma(sy, xnbt[:], xnb_d[s, row, :], reads=[b_xnbd], writes=[bxnbt])
                kb.dma(act, xfm[:].rearrange("p (c q) -> p c q", c=8),
                       xnbt[:].rearrange("p (c q) -> p c q", c=8),
                       transpose=True, reads=[bxnbt], writes=[bxfm])
                for g in range(6):
                    ps_g, b_ps = qkv_slots[g % 3]
                    for dc in range(8):
                        kb.op(pe, "matmul", ps_g, xfm[:, dc * 128:(dc + 1) * 128],
                              wqkvT[:, dc * 3 * D + g * 512: dc * 3 * D + (g + 1) * 512],
                              start=(dc == 0), stop=(dc == 7),
                              reads=[bxfm, b_wq], writes=[b_ps])
                    if g < 2:
                        dstap, dbuf = qtm[:, g * 512:(g + 1) * 512], bqtm
                    elif g < 4:
                        dstap = k_tm[:, it * D + (g - 2) * 512: it * D + (g - 1) * 512]
                        dbuf = b_ktm[it]
                    else:
                        dstap = v_tm[:, it * D + (g - 4) * 512: it * D + (g - 3) * 512]
                        dbuf = b_vtm[it]
                    kb.op(dve, "scalar_tensor_tensor", dstap, ps_g, 1.0,
                          bias_qkv[:, g * 512:(g + 1) * 512], ALU.mult, ALU.add,
                          reads=[b_ps, b_bq], writes=[dbuf])
                kb.dma(act, qfm[:].rearrange("p (c q) -> p c q", c=8),
                       qtm[:].rearrange("p (c q) -> p c q", c=8),
                       transpose=True, reads=[bqtm], writes=[bqfm])
                kb.dma(act, kfm[:, (it % 3) * D:(it % 3 + 1) * D].rearrange("p (c q) -> p c q", c=8),
                       k_tm[:, it * D:(it + 1) * D].rearrange("p (c q) -> p c q", c=8),
                       transpose=True, reads=[b_ktm[it]], writes=[b_kfm[it % 3]])
                kb.op(dve, "memset", sdyn[:], NEG, writes=[b_sdyn])
                for mi, o in enumerate(DYADIC):
                    ot = o // 128
                    if it < ot or no_dyadic:
                        continue
                    kb.op(dve, "tensor_mul", g1[:], qtm[:],
                          k_tm[:, (it - ot) * D:(it - ot + 1) * D],
                          reads=[bqtm, b_ktm[it - ot]], writes=[b_g1])
                    kb.op(dve, "reduce_sum", sdyn[:, mi * H:(mi + 1) * H],
                          g1[:].rearrange("p (h d) -> p h d", d=HD), AX.X,
                          reads=[b_g1], writes=[b_sdyn])
                kb.op(act, "activation", edyn[:], sdyn[:], AF.Exp,
                      reads=[b_sdyn], writes=[b_edyn])
                kb.op(dve, "tensor_mul", edyn[:], edyn[:], ebd[:],
                      reads=[b_edyn, b_ebd], writes=[b_edyn])
                ev = bass.AP(edyn, 0, [[edyn[:].ap[0][0], 128], [1, H], [H, ND]])
                kb.op(dve, "reduce_sum", dsum[:], ev, AX.X,
                      reads=[b_edyn], writes=[b_dsum])
                for hg in range(4):
                    ep = hg % 2
                    heads = range(4 * hg, 4 * hg + 4)
                    for hh, h in sorted(enumerate(heads), key=lambda p: p[1] % 2):
                        for c in range(WCH):
                            jt = it - 2 + c
                            reg = P1[:, hh * 512 + c * 128: hh * 512 + (c + 1) * 128]
                            if jt < 0 or (no_c0 and c == 0):
                                kb.op(dve, "memset", reg, NEG, writes=[b_P1b[hh]])
                                continue
                            pb = (h % 2) * 64
                            lhsT = kfm[pb:pb + 64,
                                       (jt % 3) * D + (h // 2) * 128:(jt % 3) * D + (h // 2) * 128 + 128]
                            rhs = qfm[pb:pb + 64, (h // 2) * 128:(h // 2) * 128 + 128]
                            kb.op(pe, "matmul", reg, lhsT, rhs, start=True, stop=True,
                                  reads=[b_kfm[jt % 3], bqfm], writes=[b_P1b[hh]])
                    pv = P1[:].rearrange("p (a r) -> p a r", r=512)[:, :, 0:WCH * 128]
                    ev4 = e4[ep][:].rearrange("p (a r) -> p a r", r=WCH * 128)
                    kb.op(act, "activation", ev4, pv, AF.Exp,
                          reads=b_P1b, writes=[b_e4[ep]])
                    ebv = ebw[:, (4 * hg) * WCH * 128:(4 * hg + 4) * WCH * 128]
                    kb.op(dve, "tensor_mul", e4[ep][:], e4[ep][:], ebv,
                          reads=[b_e4[ep], b_ebw], writes=[b_e4[ep]])
                    for hh, h in enumerate(heads):
                        valid = [c for c in range(WCH)
                                 if it - 2 + c >= 0 and not (no_c0 and c == 0)]
                        for vi, c in enumerate(valid):
                            jt = it - 2 + c
                            ech = e4[ep][:, hh * 384 + c * 128: hh * 384 + (c + 1) * 128]
                            kb.op(pe, "matmul", P3[:, h:h + 1], ech, ones_col[:],
                                  start=(vi == 0), stop=(vi == len(valid) - 1),
                                  skip_group_check=True,
                                  reads=[b_e4[ep], b_ones], writes=[b_P3])
                            kb.op(pe, "matmul", P2[:, h * HD:(h + 1) * HD], ech,
                                  v_tm[:, jt * D + h * HD: jt * D + (h + 1) * HD],
                                  start=(vi == 0), stop=(vi == len(valid) - 1),
                                  skip_group_check=True,
                                  reads=[b_e4[ep], b_vtm[jt]], writes=[b_P2])
                kb.op(dve, "tensor_add", ssum[:], P3[:, 0:H], dsum[:],
                      reads=[b_P3, b_dsum], writes=[b_ssum])
                kb.op(dve, "reciprocal", srec[tp][:], ssum[:],
                      reads=[b_ssum], writes=[b_srec[tp]])
                vd = [] if no_dyadic else [o for o in DYADIC if it >= o // 128]
                if not vd:
                    kb.op(dve, "memset", g2[tp][:], 0.0, writes=[b_g2[tp]])
                for vi, o in enumerate(vd):
                    mi = DYADIC.index(o)
                    e_b = edyn[:, mi * H:(mi + 1) * H]
                    e_bc = e_b.unsqueeze(2).broadcast_to([128, H, HD])
                    tgt, tb = (g2[tp], b_g2[tp]) if vi == 0 else (g1, b_g1)
                    kb.op(dve, "tensor_tensor",
                          tgt[:].rearrange("p (h d) -> p h d", d=HD),
                          v_tm[:, (it - o // 128) * D:(it - o // 128 + 1) * D]
                          .rearrange("p (h d) -> p h d", d=HD),
                          e_bc, ALU.mult,
                          reads=[b_vtm[it - o // 128], b_edyn], writes=[tb])
                    if vi > 0:
                        kb.op(dve, "tensor_add", g2[tp][:], g2[tp][:], g1[:],
                              reads=[b_g2[tp], b_g1], writes=[b_g2[tp]])
                srec_bc = srec[tp][:].unsqueeze(2).broadcast_to([128, H, HD])
                v3h = lambda ap: ap.rearrange("p (h d) -> p h d", d=HD)
                kb.op(dve, "tensor_tensor", v3h(flat_sb[tp][:]), v3h(P2[:]), srec_bc,
                      ALU.mult, reads=[b_P2, b_srec[tp]], writes=[b_flat[tp]])
                kb.op(dve, "tensor_tensor", v3h(g2[tp][:]), v3h(g2[tp][:]), srec_bc,
                      ALU.mult, reads=[b_g2[tp], b_srec[tp]], writes=[b_g2[tp]])
                kb.op(dve, "tensor_add", flat_sb[tp][:], flat_sb[tp][:], g2[tp][:],
                      reads=[b_flat[tp], b_g2[tp]], writes=[b_flat[tp]])
                for g in range(2):
                    for dc in range(8):
                        kb.op(pe, "matmul", P2[:, g * 512:(g + 1) * 512],
                              xfm[:, dc * 128:(dc + 1) * 128],
                              wgT[:, dc * D + g * 512: dc * D + (g + 1) * 512],
                              start=(dc == 0), stop=(dc == 7),
                              reads=[bxfm, b_wg2], writes=[b_P2])
                kb.op(dve, "tensor_add", g4[tp][:], P2[:, 0:D], bias_gate[:],
                      reads=[b_P2, b_bg], writes=[b_g4[tp]])
                kb.op(act, "activation", g4[tp][:], g4[tp][:], AF.Tanh, scale=0.5,
                      reads=[b_g4[tp]], writes=[b_g4[tp]])
                kb.op(dve, "tensor_scalar", g4[tp][:], g4[tp][:], 0.5, 0.5,
                      ALU.mult, ALU.add, reads=[b_g4[tp]], writes=[b_g4[tp]])
                kb.op(dve, "tensor_mul", fg[tp][:], flat_sb[tp][:], g4[tp][:],
                      reads=[b_flat[tp], b_g4[tp]], writes=[b_fg[tp]])
                kb.dma(act, fg_fm[tp][:].rearrange("p (c q) -> p c q", c=8),
                       fg[tp][:].rearrange("p (c q) -> p c q", c=8),
                       transpose=True, reads=[b_fg[tp]], writes=[b_fgfm[tp]])
                for g in range(2):
                    for dc in range(8):
                        kb.op(pe, "matmul", P2[:, g * 512:(g + 1) * 512],
                              fg_fm[tp][:, dc * 128:(dc + 1) * 128],
                              woT[:, dc * D + g * 512: dc * D + (g + 1) * 512],
                              start=(dc == 0), stop=(dc == 7),
                              reads=[b_fgfm[tp], b_wo], writes=[b_P2])
                kb.op(dve, "tensor_add", ptt[:], P2[:], ptt[:],
                      reads=[b_P2, bpt], writes=[bpt])
                kb.dma(sy, out_d[s, row, :], ptt[:], reads=[bpt])


_NC_CACHE = {}


def kernel(**inputs):
    maps = core_inputs(inputs, n_cores=8)
    if "nc" not in _NC_CACHE:
        _NC_CACHE["nc"] = build(n_seq=2, nt=16, only_pass1=False)
    nc = _NC_CACHE["nc"]
    res = run_bass_kernel_spmd(nc, maps, core_ids=list(range(8)))
    out = np.concatenate([np.asarray(r["out"]) for r in res.results], axis=0)
    return out.astype(np.float32)
